# version 09
# baseline (speedup 1.0000x reference)
import numpy as np
import concourse.bass as bass
import concourse.mybir as mybir
from concourse.bass_utils import run_bass_kernel_spmd
from contextlib import ExitStack
import ml_dtypes

F32 = mybir.dt.float32
BF16 = mybir.dt.bfloat16
OP = mybir.AluOpType
AF = mybir.ActivationFunctionType
AX = mybir.AxisListType.X
MAGIC = 12582912.0
C = 1024
HID = 4096
NF = C // 128
NH2 = HID // 128


class _Dummy:
    def __getattr__(self, name):
        return lambda *a, **k: None


GROUP_SEMS = {
    "ax": ["AX0", "AX1"],
    "aw": ["AW0", "AW1", "AW2", "AW3"],
    "ag": ["AG0", "AG1"],
    "bx": ["BX0", "BX1"],
    "bw": ["BW0", "BW1", "BW2", "BW3"],
    "by": ["BY0", "BY1"],
    "cc": ["CC0", "CC1"],
    "cx": ["CX0", "CX1", "CX2", "CX3"],
    "cy": ["CY0", "CY1"],
}


class _EB:

    def __init__(self, sched, sems, counts, eng, gseq, serial=False):
        self.s = sched
        self.sems = sems
        self.counts = counts
        self.eng = eng
        self.gseq = gseq
        self.serial = serial
        self.ecnt = {}
        self.T = eng if eng is not None else _Dummy()

    def dma(self, group, name=None, **kw):
        seq = self.gseq.get(group, 0)
        self.gseq[group] = seq + 1
        sl = GROUP_SEMS[group]
        s = sl[seq % len(sl)]
        auto = f"@{group}_{seq}"
        if self.eng is None:
            c = self.counts
            c[s] = c.get(s, 0) + 16
            assert auto not in self.s
            self.s[auto] = (s, c[s])
            if name is not None:
                assert name not in self.s, f"duplicate event {name}"
                self.s[name] = (s, c[s])
        else:
            if seq >= len(sl):
                self.wait(f"@{group}_{seq - len(sl)}")
            ins = self.eng.dma_start(**kw)
            ins.then_inc(self.sems[s], 16)

    def wait(self, *names):
        if self.eng is None:
            return
        for n in names:
            if n is None:
                continue
            sem, val = self.s[n]
            self.eng.wait_ge(self.sems[sem], val)

    def inc(self, ins, sem, amt, name=None):
        if self.eng is None:
            c = self.counts
            c[sem] = c.get(sem, 0) + amt
            if name is not None:
                assert name not in self.s, f"duplicate event {name}"
                self.s[name] = (sem, c[sem])
            if self.serial:
                self.ecnt[sem] = c[sem]
        else:
            if ins is None:
                raise RuntimeError(f"missing instruction for inc {name}")
            ins.then_inc(self.sems[sem], amt)
            if self.serial:
                self.ecnt[sem] = self.ecnt.get(sem, 0) + amt
                self.eng.wait_ge(self.sems[sem], self.ecnt[sem])


def build_nc(NT):
    TPC = NT * 128
    nc = bass.Bass("TRN2", target_bir_lowering=False,
                   detect_race_conditions=False)

    din = lambda n, s: nc.dram_tensor(n, s, F32, kind="ExternalInput")
    x_h = din("x", [TPC, C])
    W_h = {nm: din(nm, [C, C]) for nm in ("Wq", "Wk0", "Wk1", "Wv0", "Wv1", "Wo")}
    w1t_h = din("w1t", [C, HID])
    w2t_h = din("w2t", [HID, C])
    vrow = {nm: din(nm, [1, C]) for nm in ("bq", "bk0", "bk1", "bv0", "bv1",
                                          "bo", "lng", "lnb", "n1w")}
    n2w_h = din("n2w", [1, HID])
    acts_h = din("acts", [1, 1])
    actb_h = din("actb", [1, 1])
    y_h = nc.dram_tensor("y", [TPC, C], F32, kind="ExternalOutput")

    q_sp = nc.dram_tensor("q_sp", [TPC, C], BF16, kind="Internal")
    yp_sp = nc.dram_tensor("yp_sp", [TPC, C], F32, kind="Internal")
    w1q_sp = nc.dram_tensor("w1q_sp", [C, HID], BF16, kind="Internal")
    w2q_sp = nc.dram_tensor("w2q_sp", [HID, C], BF16, kind="Internal")
    cc_in = nc.dram_tensor("cc_in", [128, 2048], F32, kind="Internal")
    cc_out = nc.dram_tensor("cc_out", [128, 2048], F32, kind="Internal")
    red_d = nc.dram_tensor("red_d", [1, 128], F32, kind="Internal")
    cell_d = nc.dram_tensor("cell_d", [1, 1], F32, kind="Internal")

    idf_d = nc.inline_tensor(np.eye(128, dtype=np.float32), "idf_d")
    idb_d = nc.inline_tensor(np.eye(128, dtype=ml_dtypes.bfloat16), "idb_d")
    ones_d = nc.inline_tensor(np.ones((1, 128), dtype=ml_dtypes.bfloat16),
                              "ones_d")

    x_t = x_h.ap().rearrange("(n p) c -> n p c", p=128)
    y_t = y_h.ap().rearrange("(n p) c -> n p c", p=128)
    q_t = q_sp.ap().rearrange("(n p) c -> n p c", p=128)
    yp_t = yp_sp.ap().rearrange("(n p) c -> n p c", p=128)
    w1t_t = w1t_h.ap().rearrange("(n p) c -> n p c", p=128)
    w2t_t = w2t_h.ap().rearrange("(n p) c -> n p c", p=128)
    w1q_t = w1q_sp.ap().rearrange("(n p) c -> n p c", p=128)
    w2q_t = w2q_sp.ap().rearrange("(n p) c -> n p c", p=128)
    W_t = {nm: W_h[nm].ap().rearrange("(n p) c -> n p c", p=128) for nm in W_h}

    es = ExitStack()
    sems = {}
    semnames = ["AV", "AS", "AT", "BV", "BS", "BT", "BG", "CCS", "CV", "CS",
                "CT"]
    for g in GROUP_SEMS.values():
        semnames.extend(g)
    for name in semnames:
        sems[name] = es.enter_context(nc.semaphore(name))

    sb = lambda n, s, d: es.enter_context(nc.sbuf_tensor(n, s, d))
    xbuf = [sb(f"xbuf{j}", [128, C], F32) for j in range(4)]
    idb_s = sb("idb_s", [128, 128], BF16)
    rg_s = sb("rg_s", [128, NT], F32)
    mrgn_s = sb("mrgn_s", [128, NT], F32)
    invsw = [sb(f"invsw{m}", [128, 1], F32) for m in range(2)]
    swq = [sb(f"swq{m}", [128, 1], F32) for m in range(2)]
    s_pp = sb("s_pp", [128, 1], F32)
    b_pp = sb("b_pp", [128, 1], F32)
    s2_pp = sb("s2_pp", [128, 1], F32)
    sb2_pp = sb("sb2_pp", [128, 1], F32)
    nb2_pp = sb("nb2_pp", [128, 1], F32)
    row128 = sb("row128", [1, 128], F32)

    sched = {}
    counts = {}

    es_a = ExitStack()
    sba = lambda n, s, d: es_a.enter_context(nc.sbuf_tensor(n, s, d))
    W_sb = {nm: sba(nm + "_sb", [128, NF * C], BF16)
            for nm in ("Wq", "Wk0", "Wk1", "Wv0", "Wv1")}
    brow = {nm: sba(nm + "_row", [1, C], BF16)
            for nm in ("bq", "bk0", "bk1", "bv0", "bv1")}
    ones_r = sba("ones_r", [1, 128], BF16)
    xsq = sba("xsq", [128, C], F32)
    idf_s = sba("idf_s", [128, 128], F32)
    sacc = sba("sacc", [128, 2048], F32)
    sm_s = sba("sm_s", [128, NT], F32)
    ssq_s = sba("ssq_s", [128, NT], F32)
    xnbuf = [sba(f"xn{j}", [128, C], F32) for j in range(2)]
    xnfm = [sba(f"xnfm{j}", [128, C], BF16) for j in range(2)]
    qtm = [sba(f"qtm{j}", [128, C], BF16) for j in range(2)]
    ktm = [[sba(f"k{kb}tm{j}", [128, C], BF16) for j in range(2)]
           for kb in range(2)]
    vtm = [[sba(f"v{kb}tm{j}", [128, C], BF16) for j in range(2)]
           for kb in range(2)]
    wstg = [sba(f"wstg{j}", [128, 2048], F32) for j in range(2)]
    wqo = [sba(f"wqo{j}", [128, 2048], BF16) for j in range(2)]
    wzf = sba("wzf", [128, 2048], F32)
    wsum = [sba(f"wsum{m}", [128, 1], F32) for m in range(2)]
    duo = [sba(f"duo{j}", [128, 2], F32) for j in range(2)]
    sduo = [sba(f"sduo{j}", [128, 2], F32) for j in range(2)]
    tA = [sba(f"tA{j}", [128, 1], F32) for j in range(8)]

    ps_states = es_a.enter_context(nc.psum_tensor("ps_states", [128, 2048], F32))
    ps_rot = [es_a.enter_context(nc.psum_tensor(f"ps_rot{j}", [128, C], F32))
              for j in range(2)]

    pieces = []
    for f in range(NF):
        for h in range(2):
            pieces.append((0, w1t_t[f][:, 2048 * h:2048 * (h + 1)],
                           w1q_t[f][:, 2048 * h:2048 * (h + 1)], 2048))
    for h in range(NH2):
        pieces.append((1, w2t_t[h], w2q_t[h], 1024))
    NP1 = 16
    NUMEL = [C * HID, HID * C]
    p1_at = {t: [] for t in range(NT)}
    p2_at = {t: [] for t in range(NT)}
    red_at = {t: [] for t in range(NT)}
    if NT >= 32:
        for k in range(16):
            p1_at[k // 2].append(k)
        for k in range(32):
            p1_at[k // 2].append(16 + k)
        red_at[8].append(0)
        for k in range(16):
            p2_at[9 + k].append(k)
        red_at[16].append(1)
        for k in range(32):
            p2_at[min(17 + k // 2, NT - 1)].append(16 + k)
    else:
        for k in range(16):
            p1_at[0].append(k)
        red_at[1].append(0)
        for k in range(16):
            p2_at[1].append(k)
        for k in range(32):
            p1_at[1].append(16 + k)
        red_at[2].append(1)
        for k in range(32):
            p2_at[2].append(16 + k)
    stg_ord = []
    for t in range(NT):
        for k in p1_at[t]:
            stg_ord.append(("1", k))
        for k in p2_at[t]:
            stg_ord.append(("2", k))
    stg_idx = {key: n for n, key in enumerate(stg_ord)}
    stg_consume = {("1", k): f"wc1_{k}" for _, k in stg_ord if _ == "1"}
    stg_consume.update({("2", k): f"wc2_{k}" for _, k in stg_ord if _ == "2"})

    def stg_buf(kind, k):
        return stg_idx[(kind, k)] % 2

    def stg_prev(kind, k):
        n = stg_idx[(kind, k)]
        return stg_consume[stg_ord[n - 2]] if n >= 2 else None

    p2_ord = [k for kind, k in stg_ord if kind == "2"]
    p2_pos = {k: i for i, k in enumerate(p2_ord)}

    def wqo_buf(k):
        return p2_pos[k] % 2

    def wqo_prev(k):
        i = p2_pos[k]
        return f"ws2_{p2_ord[i - 2]}" if i >= 2 else None

    def body1_sync(E):
        for t in range(NT):
            if t >= 2:
                E.wait(f"A_xn{t-2}", f"A_sq{t-2}")
            E.dma("ax", f"A_x{t}", out=xbuf[t % 4][:], in_=x_t[t])

    def body1_gpsimd(E):
        g = E.T
        E.dma("aw", "ld_idf", out=idf_s[:], in_=idf_d.ap())
        E.dma("aw", "ld_idb", out=idb_s[:], in_=idb_d.ap())
        E.dma("aw", "ld_ones", out=ones_r[:], in_=ones_d.ap())
        for nm in ("bq", "bk0", "bk1", "bv0", "bv1"):
            E.dma("aw", f"ld_{nm}", out=brow[nm][:], in_=vrow[nm].ap())
        E.dma("aw", "ld_acts", out=s_pp[:],
              in_=acts_h.ap().partition_broadcast(128))
        E.dma("aw", "ld_actb", out=b_pp[:],
              in_=actb_h.ap().partition_broadcast(128))
        for nm in ("Wq", "Wk0", "Wk1", "Wv0", "Wv1"):
            for c in range(NF):
                E.dma("aw", f"ldW_{nm}_{c}",
                      out=W_sb[nm][:, C * c:C * (c + 1)], in_=W_t[nm][c])
        for t in range(NT):
            for m in red_at[t]:
                E.wait(f"wsumd_{m}")
                E.dma("aw", f"redst_{m}",
                      out=red_d.ap().rearrange("o p -> p o"), in_=wsum[m][:])
                E.wait(f"redst_{m}")
                E.dma("aw", f"redld_{m}", out=row128[:], in_=red_d.ap())
                E.wait(f"cellr_{m}")
                E.dma("aw", f"cellst_{m}", out=cell_d.ap(), in_=tA[1][0:1, :])
                E.wait(f"cellst_{m}")
                E.dma("aw", f"swqld_{m}", out=swq[m][:],
                      in_=cell_d.ap().partition_broadcast(128))
            for k in p1_at[t]:
                m, src, dst, w = pieces[k]
                E.wait(stg_prev("1", k))
                E.dma("aw", f"wl1_{k}",
                      out=wstg[stg_buf("1", k)][:, :w], in_=src)
            for k in p2_at[t]:
                m, src, dst, w = pieces[k]
                E.wait(stg_prev("2", k))
                E.dma("aw", f"wl2_{k}",
                      out=wstg[stg_buf("2", k)][:, :w], in_=src)
                E.wait(f"wc2_{k}")
                E.dma("aw", f"ws2_{k}", out=dst, in_=wqo[wqo_buf(k)][:, :w])
            E.wait(f"A_evq{t}")
            E.dma("ag", f"A_sp{t}", out=q_t[t], in_=qtm[t % 2][:])
        E.wait("sacc_evac")
        E.dma("cc", "ar_in", out=cc_in.ap(), in_=sacc[:])

    def body1_gpsimd_end(E):
        pass

    def body1_vector(E):
        v = E.T
        E.inc(v.memset(wsum[0][:], 0.0), "AV", 1)
        E.inc(v.memset(wsum[1][:], 0.0), "AV", 1)
        E.wait("ld_acts", "ld_actb")
        E.inc(v.tensor_tensor(s2_pp[:], s_pp[:], s_pp[:], OP.mult), "AV", 1)
        E.inc(v.tensor_tensor(sb2_pp[:], s_pp[:], b_pp[:], OP.mult), "AV", 1)
        E.inc(v.tensor_scalar(sb2_pp[:], sb2_pp[:], 2.0, None, OP.mult),
              "AV", 1)
        E.inc(v.tensor_tensor(nb2_pp[:], b_pp[:], b_pp[:], OP.mult), "AV", 1)
        E.inc(v.tensor_scalar(nb2_pp[:], nb2_pp[:], float(HID), None,
                              OP.mult), "AV", 1)
        for nm in ("Wq", "Wk0", "Wk1", "Wv0", "Wv1"):
            E.wait("ld_idb")
            for c in range(NF):
                E.wait(f"ldW_{nm}_{c}")
                d0 = C * c + 128 * c
                E.inc(v.tensor_tensor(W_sb[nm][:, d0:d0 + 128],
                                      W_sb[nm][:, d0:d0 + 128], idb_s[:],
                                      OP.add), "AV", 1)
        E.inc(v.engine_nop(), "AV", 1, "setup_v_done")

        def stats1(s):
            j = s % 2
            E.wait(f"A_x{s}")
            E.inc(v.reduce_sum(sm_s[:, s:s + 1], xbuf[s % 4][:], AX), "AV", 1)
            E.wait(f"A_sq{s}", None if s < 2 else f"A_sqd{s-2}")
            m, t1, mm, var, rve, msq = tA[0], tA[1], tA[2], tA[3], tA[4], tA[5]
            E.inc(v.tensor_scalar(m[:], sm_s[:, s:s + 1], 1.0 / C, None,
                                  OP.mult), "AV", 1)
            E.inc(v.tensor_scalar(t1[:], ssq_s[:, s:s + 1], 1.0 / C, None,
                                  OP.mult), "AV", 1)
            E.inc(v.tensor_tensor(mm[:], m[:], m[:], OP.mult), "AV", 1)
            E.inc(v.tensor_tensor(var[:], t1[:], mm[:], OP.subtract), "AV", 1)
            E.inc(v.tensor_scalar(duo[j][:, 0:1], var[:], 1e-5, None, OP.add),
                  "AV", 1)
            E.inc(v.reciprocal(rve[:], duo[j][:, 0:1]), "AV", 1)
            E.inc(v.tensor_tensor(msq[:], var[:], rve[:], OP.mult), "AV", 1)
            E.inc(v.tensor_scalar(duo[j][:, 1:2], msq[:], 1e-8, None, OP.add),
                  "AV", 1, f"A_duo{s}")
            E.inc(v.tensor_copy(tA[6][:], m[:]), "AV", 1)

        def stats2(s):
            j = s % 2
            E.wait(f"A_sqd{s}", None if s < 2 else f"A_tp{s-2}")
            rr, gg, mr, mrn = tA[0], tA[7], tA[2], tA[3]
            E.inc(v.reciprocal(rr[:], sduo[j][:, 0:1]), "AV", 1)
            E.inc(v.reciprocal(gg[:], sduo[j][:, 1:2]), "AV", 1)
            E.inc(v.tensor_tensor(rg_s[:, s:s + 1], rr[:], gg[:], OP.mult),
                  "AV", 1)
            E.inc(v.tensor_tensor(mr[:], tA[6][:], rr[:], OP.mult), "AV", 1)
            E.inc(v.tensor_scalar(mrn[:], mr[:], -1.0, None, OP.mult), "AV", 1)
            E.inc(v.tensor_scalar(mrgn_s[:, s:s + 1], mr[:], gg[:], -1.0,
                                  OP.mult, OP.mult), "AV", 1)
            E.inc(v.tensor_scalar(xnbuf[j][:], xbuf[s % 4][:], rr[:], mrn[:],
                                  OP.mult, OP.add), "AV", 1, f"A_xn{s}")

        stats1(0)
        stats2(0)
        for t in range(NT):
            if t + 1 < NT:
                stats1(t + 1)
            for m_ in red_at[t]:
                E.wait(f"redld_{m_}")
                E.inc(v.reduce_sum(tA[1][0:1, :], row128[:], AX), "AV", 1)
                E.inc(v.tensor_scalar(tA[1][0:1, :], tA[1][0:1, :],
                                      1.0 / NUMEL[m_], None, OP.mult), "AV", 1)
                E.inc(v.tensor_scalar(tA[1][0:1, :], tA[1][0:1, :], 1e-5,
                                      None, OP.max), "AV", 1)
                E.inc(v.reciprocal(tA[1][0:1, :], tA[1][0:1, :]), "AV", 1,
                      f"cellr_{m_}")
                E.wait(f"swqld_{m_}")
                E.inc(v.reciprocal(invsw[m_][:], swq[m_][:]), "AV", 1,
                      f"wred_{m_}")
            for k in p1_at[t]:
                m_, src, dst, w = pieces[k]
                E.wait(f"wl1_{k}")
                E.inc(v.reduce_sum(tA[1][:], wstg[stg_buf("1", k)][:, :w], AX,
                                   apply_absolute_value=True), "AV", 1)
                E.inc(v.tensor_tensor(wsum[m_][:], wsum[m_][:], tA[1][:],
                                      OP.add), "AV", 1, f"wc1_{k}")
                if k == NP1 - 1 or k == NP1 + 32 - 1:
                    E.inc(v.engine_nop(), "AV", 1,
                          f"wsumd_{0 if k < NP1 else 1}")
            for k in p2_at[t]:
                m_, src, dst, w = pieces[k]
                E.wait(f"wl2_{k}", f"wred_{m_}", wqo_prev(k))
                E.inc(v.tensor_scalar(wzf[:, :w], wstg[stg_buf("2", k)][:, :w],
                                      swq[m_][:], MAGIC, OP.mult, OP.add),
                      "AV", 1)
                E.inc(v.tensor_scalar(wzf[:, :w], wzf[:, :w], -MAGIC, None,
                                      OP.add), "AV", 1)
                E.inc(v.tensor_scalar(wqo[wqo_buf(k)][:, :w], wzf[:, :w], 1.0,
                                      -1.0, OP.min, OP.max), "AV", 1,
                      f"wc2_{k}")
            if t + 1 < NT:
                stats2(t + 1)
        E.wait(f"A_s1{NT-1}", f"A_s0{NT-1}")
        E.inc(v.tensor_copy(sacc[:, 0:1024], ps_states[:, 0:1024]), "AV", 1)
        E.inc(v.tensor_copy(sacc[:, 1024:2048], ps_states[:, 1024:2048]),
              "AV", 1, "sacc_evac")

    def body1_scalar(E):
        s = E.T
        E.wait(f"A_x0")
        E.inc(s.activation(xsq[:], xbuf[0][:], AF.Square,
                           accum_out=ssq_s[:, 0:1]), "AS", 1, "A_sq0")
        E.wait("A_duo0")
        E.inc(s.activation(sduo[0][:], duo[0][:], AF.Sqrt), "AS", 1, "A_sqd0")
        for t in range(NT + 1):
            if t + 1 < NT:
                tn = t + 1
                E.wait(f"A_x{tn}")
                E.inc(s.activation(xsq[:], xbuf[tn % 4][:], AF.Square,
                                   accum_out=ssq_s[:, tn:tn + 1]), "AS", 1,
                      f"A_sq{tn}")
            if t >= 1:
                tp = t - 1
                jp = tp % 2
                E.wait(f"A_q{tp}", None if tp < 2 else f"A_sp{tp-2}")
                E.inc(s.activation(qtm[jp][:], ps_rot[1][:], AF.Identity),
                      "AS", 1, f"A_evq{tp}")
                E.wait(f"A_k0{tp}", None if tp < 2 else f"A_s0{tp-2}")
                E.inc(s.activation(ktm[0][jp][:], ps_rot[0][:], AF.Identity),
                      "AS", 1, f"A_evk0{tp}")
                E.wait(f"A_v0{tp}", None if tp < 2 else f"A_s0{tp-2}")
                E.inc(s.activation(vtm[0][jp][:], ps_rot[1][:], AF.Identity),
                      "AS", 1, f"A_evv0{tp}")
                E.wait(f"A_k1{tp}", None if tp < 2 else f"A_s1{tp-2}")
                E.inc(s.activation(ktm[1][jp][:], ps_rot[0][:], AF.Identity),
                      "AS", 1, f"A_evk1{tp}")
                E.wait(f"A_v1{tp}", None if tp < 2 else f"A_s1{tp-2}")
                E.inc(s.activation(vtm[1][jp][:], ps_rot[1][:], AF.Identity),
                      "AS", 1, f"A_evv1{tp}")
            if t < NT:
                E.wait(f"A_tp{t}", None if t < 2 else f"A_v1{t-2}")
                E.inc(s.activation(xnfm[t % 2][:], ps_rot[0][:], AF.Identity),
                      "AS", 1, f"A_evtp{t}")
                if t + 1 < NT:
                    tn = t + 1
                    E.wait(f"A_duo{tn}", f"A_xn{tn-2}" if tn >= 2 else None)
                    E.inc(s.activation(sduo[tn % 2][:], duo[tn % 2][:],
                                       AF.Sqrt), "AS", 1, f"A_sqd{tn}")

    _BIAS = {"Wq": "bq", "Wk0": "bk0", "Wk1": "bk1", "Wv0": "bv0",
             "Wv1": "bv1"}

    def _proj(E, T, dst_ps, src_fm, Wname, t, ev):
        for f in range(NF):
            T.ldweights(src_fm[:, 128 * f:128 * (f + 1)])
            for h in range(2):
                T.matmul(dst_ps[:, 512 * h:512 * (h + 1)],
                         src_fm[:, 128 * f:128 * (f + 1)],
                         W_sb[Wname][:, C * f + 512 * h:C * f + 512 * (h + 1)],
                         start=(f == 0), stop=False)
        b = brow[_BIAS[Wname]]
        T.ldweights(ones_r[:])
        for h in range(2):
            ins = T.matmul(dst_ps[:, 512 * h:512 * (h + 1)], ones_r[:],
                           b[:, 512 * h:512 * (h + 1)],
                           start=False, stop=True)
        E.inc(ins, "AT", 1, ev)

    def body1_tensor(E):
        T = E.T
        E.wait("A_xn0", "setup_v_done", "ld_idf")
        ins = None
        for c in range(NF):
            ins = T.transpose(ps_rot[0][:, 128 * c:128 * (c + 1)],
                              xnbuf[0][:, 128 * c:128 * (c + 1)], idf_s[:])
        E.inc(ins, "AT", 1, "A_tp0")
        for u in range(1, NT + 1):
            t = u - 1
            j = t % 2
            E.wait(f"A_evtp{t}", None if t < 1 else f"A_evv1{t-1}")
            _proj(E, T, ps_rot[1], xnfm[j], "Wq", t, f"A_q{t}")
            _proj(E, T, ps_rot[0], xnfm[j], "Wk0", t, f"A_k0{t}")
            E.wait(f"A_evq{t}")
            _proj(E, T, ps_rot[1], xnfm[j], "Wv0", t, f"A_v0{t}")
            E.wait(f"A_evv0{t}")
            for p in range(8):
                T.ldweights(ktm[0][j][:, 128 * p:128 * (p + 1)])
                ins = T.matmul(ps_states[:, 128 * p:128 * (p + 1)],
                               ktm[0][j][:, 128 * p:128 * (p + 1)],
                               vtm[0][j][:, 128 * p:128 * (p + 1)],
                               start=(t == 0 and p % 4 == 0),
                               stop=(t == NT - 1), skip_group_check=True)
            E.inc(ins, "AT", 1, f"A_s0{t}")
            E.wait(f"A_evk0{t}")
            _proj(E, T, ps_rot[0], xnfm[j], "Wk1", t, f"A_k1{t}")
            _proj(E, T, ps_rot[1], xnfm[j], "Wv1", t, f"A_v1{t}")
            if t + 1 < NT:
                E.wait(f"A_xn{t+1}", f"A_evk1{t}")
                for c in range(NF):
                    ins = T.transpose(
                        ps_rot[0][:, 128 * c:128 * (c + 1)],
                        xnbuf[(t + 1) % 2][:, 128 * c:128 * (c + 1)], idf_s[:])
                E.inc(ins, "AT", 1, f"A_tp{t+1}")
            E.wait(f"A_evv1{t}")
            for p in range(8):
                T.ldweights(ktm[1][j][:, 128 * p:128 * (p + 1)])
                ins = T.matmul(ps_states[:, 1024 + 128 * p:1024 + 128 * (p + 1)],
                               ktm[1][j][:, 128 * p:128 * (p + 1)],
                               vtm[1][j][:, 128 * p:128 * (p + 1)],
                               start=(t == 0 and p % 4 == 0),
                               stop=(t == NT - 1), skip_group_check=True)
            E.inc(ins, "AT", 1, f"A_s1{t}")

    body1 = [("sync", body1_sync), ("gpsimd", body1_gpsimd),
             ("vector", body1_vector), ("scalar", body1_scalar),
             ("tensor", body1_tensor)]

    A_end = [f"A_s1{NT-1}", "sacc_evac", f"A_sqd{NT-1}",
             f"A_sp{NT-1}", f"ws2_{NP1 + 32 - 1}", f"A_x{NT-1}"]

    es_b = ExitStack()
    Bt = {}

    def alloc_b():
        sbb = lambda n, s, d: es_b.enter_context(nc.sbuf_tensor(n, s, d))
        Bt["w1q"] = sbb("w1q_sb", [128, NF * HID], BF16)
        Bt["w2q"] = sbb("w2q_sb", [128, NH2 * C], BF16)
        Bt["n1w"] = sbb("n1w_rep", [128, C], F32)
        Bt["n2w"] = sbb("n2w_rep", [128, HID], BF16)
        Bt["hact"] = sbb("hact", [128, HID], BF16)
        Bt["u"] = sbb("u_buf", [128, 1024], F32)
        Bt["xh"] = sbb("xh", [128, C], F32)
        Bt["xq"] = sbb("xq", [128, C], BF16)
        Bt["xqfm"] = sbb("xqfm", [128, C], BF16)
        Bt["xq2tm"] = [sbb(f"xq2tm{j}", [128, HID], BF16) for j in range(2)]
        Bt["xq2fm"] = sbb("xq2fm", [128, HID], BF16)
        Bt["qs1"] = [sbb(f"qs1_{j}", [128, 1], F32) for j in range(2)]
        Bt["beta"] = [sbb(f"beta{j}", [128, 1], F32) for j in range(2)]
        Bt["alpha"] = [sbb(f"alpha{j}", [128, 1], F32) for j in range(3)]
        Bt["qs2"] = sbb("qs2", [128, 1], F32)
        Bt["tB2"] = [sbb(f"tB2_{j}", [128, 1], F32) for j in range(3)]
        Bt["ug"] = sbb("ug", [128, 1024], F32)
        Bt["sqacc"] = sbb("sqacc", [128, 8], F32)
        Bt["sr2"] = sbb("sr2", [128, 1], F32)
        Bt["br2"] = sbb("br2", [128, 1], F32)
        Bt["geB"] = [sbb(f"geB{j}", [128, 1], F32) for j in range(2)]
        Bt["sgeB"] = [sbb(f"sgeB{j}", [128, 1], F32) for j in range(2)]
        Bt["tB"] = [sbb(f"tB{j}", [128, 1], F32) for j in range(4)]
        Bt["ps_h"] = [es_b.enter_context(
            nc.psum_tensor(f"ps_h{j}", [128, 1024], F32)) for j in range(2)]
        Bt["ps_tpB"] = [es_b.enter_context(
            nc.psum_tensor(f"ps_tpB{j}", [128, 1024], BF16)) for j in range(2)]
        Bt["ps_y2"] = es_b.enter_context(
            nc.psum_tensor("ps_y2", [128, 1024], F32))

    def body2_sync(E):
        s = E.T
        E.wait(*A_end)
        for f in range(NF):
            E.dma("bw", f"B_w1l{f}",
                  out=Bt["w1q"][:, HID * f:HID * (f + 1)], in_=w1q_t[f])
        for h in range(NH2):
            E.dma("bw", f"B_w2l{h}",
                  out=Bt["w2q"][:, C * h:C * (h + 1)], in_=w2q_t[h])
        for t in range(NT):
            if t >= 4:
                E.wait(f"B_yp{t-4}")
            E.dma("bx", f"B_x{t}", out=xbuf[t % 4][:], in_=x_t[t])

    def body2_gpsimd(E):
        g = E.T
        E.wait(*A_end)
        E.dma("cc", "B_n1w", out=Bt["n1w"][:],
              in_=vrow["n1w"].ap().partition_broadcast(128))
        E.dma("cc", "B_n2w", out=Bt["n2w"][:],
              in_=n2w_h.ap().partition_broadcast(128))
        E.wait("ar_in")
        E.inc(g.collective_compute(
            "AllReduce", OP.add,
            replica_groups=[[0, 1], [2, 3], [4, 5], [6, 7]],
            ins=[cc_in.ap()], outs=[cc_out.ap()]), "CCS", 1, "ar_cc")

    def _b_xq_step(E, v, tn):
        jn = tn % 2
        E.wait(f"B_x{tn}", "B_n1w",
               None if tn < 2 else f"B_relu{tn-2}_3",
               None if tn < 4 else f"B_ydma{tn-4}",
               None if tn < 1 else f"B_tpq{tn-1}")
        E.inc(v.tensor_scalar(Bt["xh"][:], xbuf[tn % 4][:], rg_s[:, tn:tn + 1],
                              mrgn_s[:, tn:tn + 1], OP.mult, OP.add), "BV", 1)
        E.inc(v.tensor_tensor(Bt["xh"][:], Bt["xh"][:], Bt["n1w"][:],
                              OP.mult), "BV", 1)
        E.inc(v.reduce_max(Bt["tB"][0][:], Bt["xh"][:], AX,
                           apply_absolute_value=True), "BV", 1)
        E.inc(v.tensor_scalar(Bt["tB"][0][:], Bt["tB"][0][:], 1e-5, None,
                              OP.max), "BV", 1)
        E.inc(v.reciprocal(Bt["tB"][1][:], Bt["tB"][0][:]), "BV", 1)
        E.inc(v.tensor_scalar(Bt["qs1"][jn][:], Bt["tB"][1][:], 127.0, None,
                              OP.mult), "BV", 1)
        E.inc(v.reciprocal(Bt["tB"][2][:], Bt["qs1"][jn][:]), "BV", 1)
        E.inc(v.tensor_tensor(Bt["beta"][jn][:], Bt["tB"][2][:], invsw[0][:],
                              OP.mult), "BV", 1)
        E.inc(v.tensor_scalar(Bt["xh"][:], Bt["xh"][:], Bt["qs1"][jn][:],
                              MAGIC, OP.mult, OP.add), "BV", 1)
        E.inc(v.tensor_scalar(Bt["xq"][:], Bt["xh"][:], -MAGIC, None,
                              OP.add), "BV", 1, f"B_xq{tn}")

    def body2_vector(E):
        v = E.T
        E.wait(*A_end)
        _b_xq_step(E, v, 0)
        E.wait("B_tpq0")
        E.inc(v.tensor_copy(Bt["xqfm"][:], Bt["ps_tpB"][0][:]), "BV", 1,
              "B_xqfm0")
        for t in range(NT + 2):
            j = t % 2
            if t >= 2 and t - 2 < NT:
                tr = t - 2
                for q in range(4):
                    E.wait(f"B_tpxq2_{tr}_{q}")
                    E.inc(v.tensor_copy(
                        Bt["xq2fm"][:, 1024 * q:1024 * (q + 1)],
                        Bt["ps_tpB"][(q + 1) % 2][:]), "BV", 1,
                        f"B_xq2fm{tr}_{q}")
            if t + 1 < NT:
                _b_xq_step(E, v, t + 1)
            if t + 1 < NT:
                E.wait(f"B_tpq{t+1}")
                E.inc(v.tensor_copy(Bt["xqfm"][:], Bt["ps_tpB"][0][:]), "BV",
                      1, f"B_xqfm{t+1}")
            if t >= 2 and t - 2 < NT:
                tr = t - 2
                E.wait(f"B_w2_{tr}", None if tr < 1 else f"B_ydma{tr-1}")
                E.inc(v.scalar_tensor_tensor(
                    Bt["xh"][:], Bt["ps_y2"][:], Bt["alpha"][tr % 3][:],
                    xbuf[tr % 4][:], OP.mult, OP.add), "BV", 1, f"B_yp{tr}")
            if t < NT:
                E.wait(f"B_sqacc{t}_3")
                E.inc(v.tensor_tensor(Bt["tB"][0][:], Bt["sqacc"][:, 0:1],
                                      Bt["sqacc"][:, 1:2], OP.add), "BV", 1)
                E.inc(v.tensor_tensor(Bt["tB"][1][:], Bt["sqacc"][:, 2:3],
                                      Bt["sqacc"][:, 3:4], OP.add), "BV", 1)
                E.inc(v.tensor_tensor(Bt["tB"][0][:], Bt["tB"][0][:],
                                      Bt["tB"][1][:], OP.add), "BV", 1)
                E.inc(v.tensor_tensor(Bt["tB"][1][:], Bt["sqacc"][:, 4:5],
                                      Bt["sqacc"][:, 5:6], OP.add), "BV", 1)
                E.inc(v.tensor_tensor(Bt["tB"][3][:], Bt["sqacc"][:, 6:7],
                                      Bt["sqacc"][:, 7:8], OP.add), "BV", 1)
                E.inc(v.tensor_tensor(Bt["tB"][1][:], Bt["tB"][1][:],
                                      Bt["tB"][3][:], OP.add), "BV", 1)
                E.inc(v.tensor_scalar(Bt["tB"][0][:], Bt["tB"][0][:],
                                      sb2_pp[:], None, OP.mult), "BV", 1)
                E.inc(v.tensor_scalar(Bt["tB"][1][:], Bt["tB"][1][:],
                                      s2_pp[:], None, OP.mult), "BV", 1)
                E.inc(v.tensor_tensor(Bt["tB"][0][:], Bt["tB"][0][:],
                                      Bt["tB"][1][:], OP.add), "BV", 1)
                E.inc(v.tensor_tensor(Bt["tB"][0][:], Bt["tB"][0][:],
                                      nb2_pp[:], OP.add), "BV", 1)
                E.inc(v.tensor_scalar(Bt["geB"][j][:], Bt["tB"][0][:],
                                      1.0 / HID, 1e-8, OP.mult, OP.add),
                      "BV", 1, f"B_ge{t}")
                E.wait(f"B_sge{t}", "B_n2w")
                E.inc(v.reciprocal(Bt["tB"][2][:], Bt["sgeB"][j][:]), "BV", 1)
                E.inc(v.tensor_scalar(Bt["sr2"][:], Bt["tB"][2][:], s_pp[:],
                                      None, OP.mult), "BV", 1)
                E.inc(v.tensor_scalar(Bt["br2"][:], Bt["tB"][2][:], b_pp[:],
                                      None, OP.mult), "BV", 1)
                for q in range(4):
                    E.inc(v.tensor_scalar(
                        Bt["hact"][:, 1024 * q:1024 * (q + 1)],
                        Bt["hact"][:, 1024 * q:1024 * (q + 1)], Bt["sr2"][:],
                        Bt["br2"][:], OP.mult, OP.add), "BV", 1)
                    E.inc(v.tensor_tensor(
                        Bt["hact"][:, 1024 * q:1024 * (q + 1)],
                        Bt["hact"][:, 1024 * q:1024 * (q + 1)],
                        Bt["n2w"][:, 1024 * q:1024 * (q + 1)], OP.mult),
                        "BV", 1)
                E.inc(v.reduce_max(Bt["tB"][0][:], Bt["hact"][:], AX,
                                   apply_absolute_value=True), "BV", 1)
                E.inc(v.tensor_scalar(Bt["tB"][0][:], Bt["tB"][0][:], 1e-5,
                                      None, OP.max), "BV", 1)
                E.inc(v.reciprocal(Bt["tB"][1][:], Bt["tB"][0][:]), "BV", 1)
                E.inc(v.tensor_scalar(Bt["qs2"][:], Bt["tB"][1][:], 127.0,
                                      None, OP.mult), "BV", 1)
                E.inc(v.reciprocal(Bt["tB"][3][:], Bt["qs2"][:]), "BV", 1)
                E.inc(v.tensor_tensor(Bt["alpha"][t % 3][:], Bt["tB"][3][:],
                                      invsw[1][:], OP.mult), "BV", 1)
                for q in range(4):
                    E.inc(v.tensor_scalar(
                        Bt["ug"][:], Bt["hact"][:, 1024 * q:1024 * (q + 1)],
                        Bt["qs2"][:], MAGIC, OP.mult, OP.add), "BV", 1)
                    E.inc(v.tensor_scalar(
                        Bt["xq2tm"][t % 2][:, 1024 * q:1024 * (q + 1)],
                        Bt["ug"][:], -MAGIC, None, OP.add), "BV", 1,
                        f"B_xq2_{t}" if q == 3 else None)

    def body2_scalar(E):
        s = E.T
        E.wait(*A_end)
        for t in range(NT + 2):
            j = t % 2
            if t < NT:
                for q in range(4):
                    E.wait(f"B_w1_{t}_{q}",
                           f"B_xq2_{t-1}" if (q == 0 and t >= 1) else None)
                    E.inc(s.activation(Bt["u"][:], Bt["ps_h"][q % 2][:],
                                       AF.Relu, scale=Bt["beta"][j][:]),
                          "BS", 1, f"B_relu{t}_{q}")
                    E.wait(None if t < 1 else f"B_xq2_{t-1}")
                    E.inc(s.activation(Bt["hact"][:, 1024 * q:1024 * (q + 1)],
                                       Bt["u"][:], AF.Square,
                                       accum_out=Bt["sqacc"][:, q:q + 1]),
                          "BS", 1)
                    E.inc(s.activation(Bt["u"][:],
                                       Bt["hact"][:, 1024 * q:1024 * (q + 1)],
                                       AF.Square,
                                       accum_out=Bt["sqacc"][:, 4 + q:5 + q]),
                          "BS", 1, f"B_sqacc{t}_{q}")
                E.wait(f"B_ge{t}")
                E.inc(s.activation(Bt["sgeB"][j][:], Bt["geB"][j][:], AF.Sqrt),
                      "BS", 1, f"B_sge{t}")
            if t >= 2 and t - 2 < NT:
                tr = t - 2
                E.wait(f"B_yp{tr}")
                E.dma("by", f"B_ydma{tr}", out=yp_t[tr], in_=Bt["xh"][:])

    def _b_w2(E, T, t):
        E.wait(f"B_xq2fm{t}_3", None if t < 1 else f"B_yp{t-1}",
               f"B_w2l{NH2-1}" if t == 0 else None)
        for h in range(NH2):
            T.ldweights(Bt["xq2fm"][:, 128 * h:128 * (h + 1)])
            for g2 in range(2):
                ins = T.matmul(
                    Bt["ps_y2"][:, 512 * g2:512 * (g2 + 1)],
                    Bt["xq2fm"][:, 128 * h:128 * (h + 1)],
                    Bt["w2q"][:, C * h + 512 * g2:C * h + 512 * (g2 + 1)],
                    start=(h == 0), stop=(h == NH2 - 1))
        E.inc(ins, "BT", 1, f"B_w2_{t}")

    def body2_tensor(E):
        T = E.T
        E.wait(*A_end, f"B_w1l{NF-1}")
        for t in range(NT + 2):
            if t < NT:
                E.wait(f"B_xq{t}",
                       None if t < 3 else f"B_xq2fm{t-3}_3")
                for c in range(NF):
                    ins = T.transpose(Bt["ps_tpB"][0][:, 128 * c:128 * (c + 1)],
                                      Bt["xq"][:, 128 * c:128 * (c + 1)],
                                      idb_s[:])
                E.inc(ins, "BT", 1, f"B_tpq{t}")
            if t >= 2 and t - 2 < NT:
                tr = t - 2
                for q in range(4):
                    bk = (q + 1) % 2
                    if q == 0:
                        E.wait(f"B_xq2_{tr}",
                               None if tr < 1 else f"B_xq2fm{tr-1}_2")
                    elif q == 1:
                        E.wait(f"B_xqfm{t}" if t < NT
                               else f"B_xq2fm{tr-1}_3")
                    else:
                        E.wait(f"B_xq2fm{tr}_{q-2}")
                    for c in range(8):
                        ins = T.transpose(
                            Bt["ps_tpB"][bk][:, 128 * c:128 * (c + 1)],
                            Bt["xq2tm"][tr % 2][:, 1024 * q + 128 * c:
                                                1024 * q + 128 * (c + 1)],
                            idb_s[:])
                    E.inc(ins, "BT", 1, f"B_tpxq2_{tr}_{q}")
            if t < NT:
                for q in range(4):
                    E.wait(f"B_xqfm{t}" if q == 0 else None,
                           (f"B_relu{t}_{q-2}" if q >= 2 else
                            (f"B_relu{t-1}_{q+2}" if t >= 1 else None)))
                    for f in range(NF):
                        T.ldweights(Bt["xqfm"][:, 128 * f:128 * (f + 1)])
                        for g2 in range(2):
                            ins = T.matmul(
                                Bt["ps_h"][q % 2][:, 512 * g2:512 * (g2 + 1)],
                                Bt["xqfm"][:, 128 * f:128 * (f + 1)],
                                Bt["w1q"][:, HID * f + 1024 * q + 512 * g2:
                                           HID * f + 1024 * q + 512 * (g2 + 1)],
                                start=(f == 0), stop=(f == NF - 1))
                    E.inc(ins, "BT", 1, f"B_w1_{t}_{q}")
            if t >= 2 and t - 2 < NT:
                _b_w2(E, T, t - 2)

    body2 = [("sync", body2_sync), ("gpsimd", body2_gpsimd),
             ("vector", body2_vector), ("scalar", body2_scalar),
             ("tensor", body2_tensor)]

    B_end = [f"B_w2_{NT-1}", f"B_yp{NT-1}", f"B_ydma{NT-1}",
             f"B_sge{NT-1}", f"B_x{NT-1}"]

    es_c = ExitStack()
    Ct = {}

    def alloc_c():
        sbc = lambda n, s, d: es_c.enter_context(nc.sbuf_tensor(n, s, d))
        Ct["Wo"] = sbc("Wo_sb", [128, NF * C], BF16)
        Ct["lng"] = sbc("lng_rep", [128, C], F32)
        Ct["lnb"] = sbc("lnb_rep", [128, C], F32)
        Ct["bo"] = sbc("bo_rep", [128, C], F32)
        Ct["qb"] = [sbc(f"qbC{j}", [128, C], BF16) for j in range(2)]
        Ct["ypb"] = [sbc(f"ypbC{j}", [128, C], F32) for j in range(2)]
        Ct["qfm"] = sbc("qfmC", [128, C], BF16)
        Ct["ot"] = [sbc(f"otC{j}", [128, C], F32) for j in range(2)]
        Ct["osc"] = sbc("oscC", [128, C], F32)
        Ct["oln"] = [sbc(f"olnC{j}", [128, C], BF16) for j in range(2)]
        Ct["olnfm"] = sbc("olnfmC", [128, C], BF16)
        Ct["yb"] = [sbc(f"ybC{j}", [128, C], F32) for j in range(2)]
        Ct["so"] = sbc("soC", [128, 1], F32)
        Ct["so2"] = [sbc(f"so2C{j}", [128, 1], F32) for j in range(2)]
        Ct["ve"] = [sbc(f"veC{j}", [128, 1], F32) for j in range(2)]
        Ct["sve"] = [sbc(f"sveC{j}", [128, 1], F32) for j in range(2)]
        Ct["tC"] = [sbc(f"tC{j}", [128, 1], F32) for j in range(4)]
        Ct["sres"] = sbc("sresC", [128, 2048], F32)
        Ct["state"] = sbc("state_sb", [128, C], BF16)
        Ct["ps_tp"] = [es_c.enter_context(
            nc.psum_tensor(f"ps_tpC{j}", [128, C], BF16)) for j in range(2)]
        Ct["ps_out"] = es_c.enter_context(
            nc.psum_tensor("ps_outC", [128, C], F32))
        Ct["ps_yw"] = es_c.enter_context(
            nc.psum_tensor("ps_ywC", [128, C], F32))

    def body3_sync(E):
        s = E.T
        E.wait(*B_end)
        for t in range(NT):
            if t >= 2:
                E.wait(f"C_qfm{t-2}")
            E.dma("cx", f"C_ql{t}", out=Ct["qb"][t % 2][:], in_=q_t[t])
            if t >= 2:
                E.wait(f"C_y{t-2}")
            E.dma("cx", f"C_ypl{t}", out=Ct["ypb"][t % 2][:], in_=yp_t[t])
        E.wait(f"C_ydma{NT-1}", f"C_ydma{NT-2}")

    def body3_gpsimd(E):
        g = E.T
        E.wait(*B_end, "ar_cc")
        E.dma("cc", "ar_done", out=Ct["sres"][:], in_=cc_out.ap())
        for c in range(NF):
            E.dma("aw", f"C_Wo{c}", out=Ct["Wo"][:, C * c:C * (c + 1)],
                  in_=W_t["Wo"][c])
        for nm in ("lng", "lnb", "bo"):
            E.dma("aw", f"C_{nm}", out=Ct[nm][:],
                  in_=vrow[nm].ap().partition_broadcast(128))
        for t in range(NT):
            E.wait(f"C_y{t}")
            E.dma("cy", f"C_ydma{t}", out=y_t[t], in_=Ct["yb"][t % 2][:])

    def body3_vector(E):
        v = E.T
        E.wait(*B_end, "C_lng", "C_lnb", "C_bo", "ar_done")
        E.inc(v.tensor_tensor(Ct["sres"][:, 0:1024], Ct["sres"][:, 0:1024],
                              Ct["sres"][:, 1024:2048], OP.mult), "CV", 1)
        E.inc(v.tensor_scalar(Ct["state"][:], Ct["sres"][:, 0:1024],
                              1.0 / 16777216.0, None, OP.mult), "CV", 1)
        ins = None
        for p in range(8):
            v.memset(Ct["state"][0:64, 128 * p + 64:128 * (p + 1)], 0.0)
            ins = v.memset(Ct["state"][64:128, 128 * p:128 * p + 64], 0.0)
        E.inc(ins, "CV", 1, "state_ready")
        for t in range(NT + 2):
            if t >= 1 and t - 1 < NT:
                s_ = t - 1
                j = s_ % 2
                E.wait(f"C_ot{s_}")
                E.inc(v.reduce_sum(Ct["so"][:], Ct["ot"][j][:], AX), "CV", 1)
                E.wait(f"C_sq{s_}")
                m2, vv_, mm2 = Ct["tC"][0], Ct["tC"][1], Ct["tC"][2]
                E.inc(v.tensor_scalar(m2[:], Ct["so"][:], 1.0 / C, None,
                                      OP.mult), "CV", 1)
                E.inc(v.tensor_scalar(vv_[:], Ct["so2"][j][:], 1.0 / C, None,
                                      OP.mult), "CV", 1)
                E.inc(v.tensor_tensor(mm2[:], m2[:], m2[:], OP.mult), "CV", 1)
                E.inc(v.tensor_tensor(vv_[:], vv_[:], mm2[:], OP.subtract),
                      "CV", 1)
                E.inc(v.tensor_scalar(Ct["ve"][j][:], vv_[:], 1e-5, None,
                                      OP.add), "CV", 1, f"C_ve{s_}")
                E.wait(f"C_sve{s_}")
                rstd, mn = Ct["tC"][1], Ct["tC"][2]
                E.inc(v.reciprocal(rstd[:], Ct["sve"][j][:]), "CV", 1)
                E.inc(v.tensor_scalar(mn[:], m2[:], rstd[:], -1.0, OP.mult,
                                      OP.mult), "CV", 1)
                E.inc(v.tensor_scalar(Ct["ot"][j][:], Ct["ot"][j][:], rstd[:],
                                      mn[:], OP.mult, OP.add), "CV", 1)
                E.inc(v.tensor_tensor(Ct["ot"][j][:], Ct["ot"][j][:],
                                      Ct["lng"][:], OP.mult), "CV", 1)
                E.inc(v.tensor_tensor(Ct["oln"][j][:], Ct["ot"][j][:],
                                      Ct["lnb"][:], OP.add), "CV", 1,
                      f"C_oln{s_}")
            if t >= 2:
                tr = t - 2
                E.wait(f"C_wo{tr}", f"C_ypl{tr}",
                       None if tr < 2 else f"C_ydma{tr-2}")
                jp = tr % 2
                E.inc(v.tensor_tensor(Ct["yb"][jp][:], Ct["ps_yw"][:],
                                      Ct["ypb"][jp][:], OP.add), "CV", 1)
                E.inc(v.tensor_tensor(Ct["yb"][jp][:], Ct["yb"][jp][:],
                                      Ct["bo"][:], OP.add), "CV", 1,
                      f"C_y{tr}")

    def body3_scalar(E):
        s = E.T
        E.wait(*B_end)
        for t in range(NT + 2):
            if t >= 1 and t - 1 < NT:
                s_ = t - 1
                E.wait(f"C_ve{s_}")
                E.inc(s.activation(Ct["sve"][s_ % 2][:], Ct["ve"][s_ % 2][:],
                                   AF.Sqrt), "CS", 1, f"C_sve{s_}")
            if t < NT:
                E.wait(f"C_tpq{t}")
                E.inc(s.activation(Ct["qfm"][:], Ct["ps_tp"][0][:],
                                   AF.Identity), "CS", 1, f"C_qfm{t}")
                j = t % 2
                E.wait(f"C_out{t}", None if t < 2 else f"C_oln{t-2}")
                E.inc(s.activation(Ct["ot"][j][:], Ct["ps_out"][:],
                                   AF.Identity), "CS", 1, f"C_ot{t}")
                E.wait(None if t < 2 else f"C_ve{t-2}")
                E.inc(s.activation(Ct["osc"][:], Ct["ot"][j][:], AF.Square,
                                   accum_out=Ct["so2"][j][:]), "CS", 1,
                      f"C_sq{t}")
            if t >= 2:
                tr = t - 2
                E.wait(f"C_tpln{tr}")
                E.inc(s.activation(Ct["olnfm"][:], Ct["ps_tp"][1][:],
                                   AF.Identity), "CS", 1, f"C_olnfm{tr}")

    def body3_tensor(E):
        T = E.T
        E.wait(*B_end, f"C_Wo{NF-1}")
        for t in range(NT + 2):
            if t < NT:
                E.wait(f"C_ql{t}", None if t < 1 else f"C_qfm{t-1}")
                for c in range(NF):
                    ins = T.transpose(Ct["ps_tp"][0][:, 128 * c:128 * (c + 1)],
                                      Ct["qb"][t % 2][:, 128 * c:128 * (c + 1)],
                                      idb_s[:])
                E.inc(ins, "CT", 1, f"C_tpq{t}")
                E.wait(f"C_qfm{t}", "state_ready",
                       None if t < 1 else f"C_ot{t-1}")
                for p in range(8):
                    T.ldweights(Ct["qfm"][:, 128 * p:128 * (p + 1)])
                    ins = T.matmul(Ct["ps_out"][:, 128 * p:128 * (p + 1)],
                                   Ct["qfm"][:, 128 * p:128 * (p + 1)],
                                   Ct["state"][:, 128 * p:128 * (p + 1)],
                                   start=True, stop=True)
                E.inc(ins, "CT", 1, f"C_out{t}")
            if t >= 2:
                tr = t - 2
                E.wait(f"C_oln{tr}", None if tr < 1 else f"C_olnfm{tr-1}")
                for c in range(NF):
                    ins = T.transpose(Ct["ps_tp"][1][:, 128 * c:128 * (c + 1)],
                                      Ct["oln"][tr % 2][:, 128 * c:128 * (c + 1)],
                                      idb_s[:])
                E.inc(ins, "CT", 1, f"C_tpln{tr}")
                E.wait(f"C_olnfm{tr}", None if tr < 3 else f"C_y{tr-3}")
                for c in range(NF):
                    T.ldweights(Ct["olnfm"][:, 128 * c:128 * (c + 1)])
                    for h in range(2):
                        ins = T.matmul(
                            Ct["ps_yw"][:, 512 * h:512 * (h + 1)],
                            Ct["olnfm"][:, 128 * c:128 * (c + 1)],
                            Ct["Wo"][:, C * c + 512 * h:C * c + 512 * (h + 1)],
                            start=(c == 0), stop=(c == NF - 1))
                E.inc(ins, "CT", 1, f"C_wo{tr}")

    body3 = [("sync", body3_sync), ("gpsimd", body3_gpsimd),
             ("vector", body3_vector), ("scalar", body3_scalar),
             ("tensor", body3_tensor)]

    class _Slc:
        def __getitem__(self, k):
            return self

    for key in ("w1q", "w2q", "n1w", "n2w", "hact", "u", "xh", "xq",
                "xqfm", "xq2tm", "xq2fm", "qs1", "beta", "alpha", "tB2", "ug", "sr2", "br2",
                "qs2", "sqacc", "geB", "sgeB", "tB", "ps_h", "ps_tpB",
                "ps_y2"):
        Bt[key] = _Slc()
    for key in ("Wo", "lng", "lnb", "bo", "qb", "ypb", "qfm", "ot", "osc",
                "oln", "olnfm", "yb", "so", "so2", "ve", "sve", "tC", "sres", "state",
                "ps_tp", "ps_out", "ps_yw"):
        Ct[key] = _Slc()
    gseq_plan = {}
    for nm, fn in body1 + body2 + body3:
        fn(_EB(sched, sems, counts, None, gseq_plan,
               serial=nm in ("vector", "scalar", "gpsimd")))
    Bt.clear()
    Ct.clear()

    gseq_emit = {}
    with nc.Block("blk1") as blk:
        for nm, fn in body1:
            getattr(blk, nm)(lambda eng, fn=fn, nm=nm: fn(
                _EB(sched, sems, None, eng, gseq_emit,
                    serial=nm in ("vector", "scalar", "gpsimd"))))
    es_a.close()
    alloc_b()
    with nc.Block("blk2") as blk:
        for nm, fn in body2:
            getattr(blk, nm)(lambda eng, fn=fn, nm=nm: fn(
                _EB(sched, sems, None, eng, gseq_emit,
                    serial=nm in ("vector", "scalar", "gpsimd"))))
    es_b.close()
    alloc_c()
    with nc.Block("blk3") as blk:
        for nm, fn in body3:
            getattr(blk, nm)(lambda eng, fn=fn, nm=nm: fn(
                _EB(sched, sems, None, eng, gseq_emit,
                    serial=nm in ("vector", "scalar", "gpsimd"))))
    es_c.close()
    es.close()
    return nc


def kernel(L0_attn_x, L0_attn_Wq, L0_attn_bq, L0_attn_Wk, L0_attn_bk,
           L0_attn_Wv, L0_attn_bv, L0_attn_Wo, L0_attn_bo, L0_attn_ln_g,
           L0_attn_ln_b, L0_ffn_norm1_w, L0_ffn_w1, L0_ffn_act_scale,
           L0_ffn_act_bias, L0_ffn_norm2_w, L0_ffn_w2):
    NT = 32
    TPC = NT * 128
    x = np.ascontiguousarray(np.asarray(L0_attn_x, dtype=np.float32)
                             ).reshape(8 * TPC, C)
    f32 = lambda a: np.ascontiguousarray(np.asarray(a, dtype=np.float32))
    row = lambda a: f32(a).reshape(1, -1)
    common = {
        "Wq": f32(L0_attn_Wq), "Wo": f32(L0_attn_Wo),
        "Wk0": f32(np.asarray(L0_attn_Wk)[0]),
        "Wk1": f32(np.asarray(L0_attn_Wk)[1]),
        "Wv0": f32(np.asarray(L0_attn_Wv)[0]),
        "Wv1": f32(np.asarray(L0_attn_Wv)[1]),
        "w1t": f32(np.asarray(L0_ffn_w1).T),
        "w2t": f32(np.asarray(L0_ffn_w2).T),
        "bq": row(L0_attn_bq), "bk0": row(np.asarray(L0_attn_bk)[0]),
        "bk1": row(np.asarray(L0_attn_bk)[1]),
        "bv0": row(np.asarray(L0_attn_bv)[0]),
        "bv1": row(np.asarray(L0_attn_bv)[1]), "bo": row(L0_attn_bo),
        "lng": row(L0_attn_ln_g), "lnb": row(L0_attn_ln_b),
        "n1w": row(L0_ffn_norm1_w), "n2w": row(L0_ffn_norm2_w),
        "acts": f32(L0_ffn_act_scale).reshape(1, 1),
        "actb": f32(L0_ffn_act_bias).reshape(1, 1),
    }
    in_maps = [dict(common, x=np.ascontiguousarray(x[c * TPC:(c + 1) * TPC]))
               for c in range(8)]
    nc = build_nc(NT)
    res = run_bass_kernel_spmd(nc, in_maps, core_ids=list(range(8)))
    y = np.concatenate([np.asarray(res.results[c]["y"]) for c in range(8)],
                       axis=0)
    return y.reshape(4, 8192, C)


# version 10
# speedup vs baseline: 1.0575x; 1.0575x over previous
import numpy as np
import concourse.bass as bass
import concourse.mybir as mybir
from concourse.bass_utils import run_bass_kernel_spmd
from contextlib import ExitStack
import ml_dtypes

F32 = mybir.dt.float32
BF16 = mybir.dt.bfloat16
OP = mybir.AluOpType
AF = mybir.ActivationFunctionType
AX = mybir.AxisListType.X
MAGIC = 12582912.0
C = 1024
HID = 4096
NF = C // 128
NH2 = HID // 128


class _Dummy:
    def __getattr__(self, name):
        return lambda *a, **k: None


GROUP_SEMS = {
    "ax": ["AX0", "AX1"],
    "aw": ["AW0", "AW1", "AW2", "AW3"],
    "ag": ["AG0", "AG1"],
    "bx": ["BX0", "BX1"],
    "bw": ["BW0", "BW1", "BW2", "BW3"],
    "by": ["BY0", "BY1"],
    "cc": ["CC0", "CC1"],
    "cx": ["CX0", "CX1", "CX2", "CX3"],
    "cy": ["CY0", "CY1"],
}


class _EB:

    def __init__(self, sched, sems, counts, eng, gseq, serial=False):
        self.s = sched
        self.sems = sems
        self.counts = counts
        self.eng = eng
        self.gseq = gseq
        self.serial = serial
        self.ecnt = {}
        self.T = eng if eng is not None else _Dummy()

    def dma(self, group, name=None, **kw):
        seq = self.gseq.get(group, 0)
        self.gseq[group] = seq + 1
        sl = GROUP_SEMS[group]
        s = sl[seq % len(sl)]
        auto = f"@{group}_{seq}"
        if self.eng is None:
            c = self.counts
            c[s] = c.get(s, 0) + 16
            assert auto not in self.s
            self.s[auto] = (s, c[s])
            if name is not None:
                assert name not in self.s, f"duplicate event {name}"
                self.s[name] = (s, c[s])
        else:
            if seq >= len(sl):
                self.wait(f"@{group}_{seq - len(sl)}")
            ins = self.eng.dma_start(**kw)
            ins.then_inc(self.sems[s], 16)

    def wait(self, *names):
        if self.eng is None:
            return
        for n in names:
            if n is None:
                continue
            sem, val = self.s[n]
            self.eng.wait_ge(self.sems[sem], val)

    def inc(self, ins, sem, amt, name=None):
        if self.eng is None:
            c = self.counts
            c[sem] = c.get(sem, 0) + amt
            if name is not None:
                assert name not in self.s, f"duplicate event {name}"
                self.s[name] = (sem, c[sem])
            if self.serial:
                self.ecnt[sem] = c[sem]
        else:
            if ins is None:
                raise RuntimeError(f"missing instruction for inc {name}")
            ins.then_inc(self.sems[sem], amt)
            if self.serial:
                self.ecnt[sem] = self.ecnt.get(sem, 0) + amt
                self.eng.wait_ge(self.sems[sem], self.ecnt[sem])


def build_nc(NT):
    TPC = NT * 128
    nc = bass.Bass("TRN2", target_bir_lowering=False,
                   detect_race_conditions=False)

    din = lambda n, s: nc.dram_tensor(n, s, F32, kind="ExternalInput")
    x_h = din("x", [TPC, C])
    W_h = {nm: din(nm, [C, C]) for nm in ("Wq", "Wk0", "Wk1", "Wv0", "Wv1", "Wo")}
    w1t_h = din("w1t", [C, HID])
    w2t_h = din("w2t", [HID, C])
    vrow = {nm: din(nm, [1, C]) for nm in ("bq", "bk0", "bk1", "bv0", "bv1",
                                          "bo", "lng", "lnb", "n1w")}
    n2w_h = din("n2w", [1, HID])
    acts_h = din("acts", [1, 1])
    actb_h = din("actb", [1, 1])
    y_h = nc.dram_tensor("y", [TPC, C], F32, kind="ExternalOutput")

    q_sp = nc.dram_tensor("q_sp", [TPC, C], BF16, kind="Internal")
    yp_sp = nc.dram_tensor("yp_sp", [TPC, C], F32, kind="Internal")
    w1q_sp = nc.dram_tensor("w1q_sp", [C, HID], BF16, kind="Internal")
    w2q_sp = nc.dram_tensor("w2q_sp", [HID, C], BF16, kind="Internal")
    cc_in = nc.dram_tensor("cc_in", [128, 2048], F32, kind="Internal")
    cc_out = nc.dram_tensor("cc_out", [128, 2048], F32, kind="Internal")
    red_d = nc.dram_tensor("red_d", [1, 128], F32, kind="Internal")
    cell_d = nc.dram_tensor("cell_d", [1, 1], F32, kind="Internal")

    idf_d = nc.inline_tensor(np.eye(128, dtype=np.float32), "idf_d")
    idb_d = nc.inline_tensor(np.eye(128, dtype=ml_dtypes.bfloat16), "idb_d")
    ones_d = nc.inline_tensor(np.ones((1, 128), dtype=ml_dtypes.bfloat16),
                              "ones_d")

    x_t = x_h.ap().rearrange("(n p) c -> n p c", p=128)
    y_t = y_h.ap().rearrange("(n p) c -> n p c", p=128)
    q_t = q_sp.ap().rearrange("(n p) c -> n p c", p=128)
    yp_t = yp_sp.ap().rearrange("(n p) c -> n p c", p=128)
    w1t_t = w1t_h.ap().rearrange("(n p) c -> n p c", p=128)
    w2t_t = w2t_h.ap().rearrange("(n p) c -> n p c", p=128)
    w1q_t = w1q_sp.ap().rearrange("(n p) c -> n p c", p=128)
    w2q_t = w2q_sp.ap().rearrange("(n p) c -> n p c", p=128)
    W_t = {nm: W_h[nm].ap().rearrange("(n p) c -> n p c", p=128) for nm in W_h}

    es = ExitStack()
    sems = {}
    semnames = ["AV", "AS", "AT", "BV", "BS", "BT", "CCS", "CV", "CS", "CT"]
    for g in GROUP_SEMS.values():
        semnames.extend(g)
    for name in semnames:
        sems[name] = es.enter_context(nc.semaphore(name))

    sb = lambda n, s, d: es.enter_context(nc.sbuf_tensor(n, s, d))
    xbuf = [sb(f"xbuf{j}", [128, C], F32) for j in range(4)]
    idb_s = sb("idb_s", [128, 128], BF16)
    rg_s = sb("rg_s", [128, NT], F32)
    mrgn_s = sb("mrgn_s", [128, NT], F32)
    invsw = [sb(f"invsw{m}", [128, 1], F32) for m in range(2)]
    swq = [sb(f"swq{m}", [128, 1], F32) for m in range(2)]
    s_pp = sb("s_pp", [128, 1], F32)
    b_pp = sb("b_pp", [128, 1], F32)
    row128 = sb("row128", [1, 128], F32)

    sched = {}
    counts = {}

    es_a = ExitStack()
    sba = lambda n, s, d: es_a.enter_context(nc.sbuf_tensor(n, s, d))
    W_sb = {nm: sba(nm + "_sb", [128, NF * C], BF16)
            for nm in ("Wq", "Wk0", "Wk1", "Wv0", "Wv1")}
    brow = {nm: sba(nm + "_row", [1, C], BF16)
            for nm in ("bq", "bk0", "bk1", "bv0", "bv1")}
    ones_r = sba("ones_r", [1, 128], BF16)
    xsq = sba("xsq", [128, C], F32)
    idf_s = sba("idf_s", [128, 128], F32)
    sacc = sba("sacc", [128, 2048], F32)
    sm_s = sba("sm_s", [128, NT], F32)
    ssq_s = sba("ssq_s", [128, NT], F32)
    xnbuf = [sba(f"xn{j}", [128, C], F32) for j in range(2)]
    xnfm = [sba(f"xnfm{j}", [128, C], BF16) for j in range(2)]
    qtm = [sba(f"qtm{j}", [128, C], BF16) for j in range(2)]
    ktm = [[sba(f"k{kb}tm{j}", [128, C], BF16) for j in range(2)]
           for kb in range(2)]
    vtm = [[sba(f"v{kb}tm{j}", [128, C], BF16) for j in range(2)]
           for kb in range(2)]
    wstg = [sba(f"wstg{j}", [128, 2048], F32) for j in range(2)]
    wqo = [sba(f"wqo{j}", [128, 2048], BF16) for j in range(2)]
    wzf = sba("wzf", [128, 2048], F32)
    wsum = sba("wsum", [128, 1], F32)
    duo = [sba(f"duo{j}", [128, 2], F32) for j in range(2)]
    sduo = [sba(f"sduo{j}", [128, 2], F32) for j in range(2)]
    tA = [sba(f"tA{j}", [128, 1], F32) for j in range(8)]

    ps_states = es_a.enter_context(nc.psum_tensor("ps_states", [128, 2048], F32))
    ps_rot = [es_a.enter_context(nc.psum_tensor(f"ps_rot{j}", [128, C], F32))
              for j in range(2)]

    pieces = []
    for f in range(NF):
        for h in range(2):
            pieces.append((0, w1t_t[f][:, 2048 * h:2048 * (h + 1)],
                           w1q_t[f][:, 2048 * h:2048 * (h + 1)], 2048))
    for h in range(NH2):
        pieces.append((1, w2t_t[h], w2q_t[h], 1024))
    NP1 = 16
    NUMEL = [C * HID, HID * C]
    p1_at = {t: [] for t in range(NT)}
    p2_at = {t: [] for t in range(NT)}
    red_at = {t: [] for t in range(NT)}
    if NT >= 32:
        for k in range(16):
            p1_at[k // 2].append(k)
        red_at[8].append(0)
        for k in range(16):
            p2_at[9 + k // 2].append(k)
        for k in range(32):
            p1_at[9 + k // 4].append(16 + k)
        red_at[17].append(1)
        for k in range(32):
            p2_at[18 + k // 3].append(16 + k)
    else:
        for k in range(16):
            p1_at[0].append(k)
        red_at[1].append(0)
        for k in range(16):
            p2_at[1].append(k)
        for k in range(32):
            p1_at[1].append(16 + k)
        red_at[2].append(1)
        for k in range(32):
            p2_at[2].append(16 + k)
    stg_ord = []
    for t in range(NT):
        for k in p1_at[t]:
            stg_ord.append(("1", k))
        for k in p2_at[t]:
            stg_ord.append(("2", k))
    stg_idx = {key: n for n, key in enumerate(stg_ord)}
    stg_consume = {("1", k): f"wc1_{k}" for _, k in stg_ord if _ == "1"}
    stg_consume.update({("2", k): f"wc2_{k}" for _, k in stg_ord if _ == "2"})

    def stg_buf(kind, k):
        return stg_idx[(kind, k)] % 2

    def stg_prev(kind, k):
        n = stg_idx[(kind, k)]
        return stg_consume[stg_ord[n - 2]] if n >= 2 else None

    p2_ord = [k for kind, k in stg_ord if kind == "2"]
    p2_pos = {k: i for i, k in enumerate(p2_ord)}

    def wqo_buf(k):
        return p2_pos[k] % 2

    def wqo_prev(k):
        i = p2_pos[k]
        return f"ws2_{p2_ord[i - 2]}" if i >= 2 else None

    def body1_sync(E):
        for t in range(NT):
            if t >= 2:
                E.wait(f"A_xn{t-2}", f"A_sq{t-2}")
            E.dma("ax", f"A_x{t}", out=xbuf[t % 4][:], in_=x_t[t])

    def body1_gpsimd(E):
        g = E.T
        E.dma("aw", "ld_idf", out=idf_s[:], in_=idf_d.ap())
        E.dma("aw", "ld_idb", out=idb_s[:], in_=idb_d.ap())
        E.dma("aw", "ld_ones", out=ones_r[:], in_=ones_d.ap())
        for nm in ("bq", "bk0", "bk1", "bv0", "bv1"):
            E.dma("aw", f"ld_{nm}", out=brow[nm][:], in_=vrow[nm].ap())
        E.dma("aw", "ld_acts", out=s_pp[:],
              in_=acts_h.ap().partition_broadcast(128))
        E.dma("aw", "ld_actb", out=b_pp[:],
              in_=actb_h.ap().partition_broadcast(128))
        for nm in ("Wq", "Wk0", "Wk1", "Wv0", "Wv1"):
            for c in range(NF):
                E.dma("aw", f"ldW_{nm}_{c}",
                      out=W_sb[nm][:, C * c:C * (c + 1)], in_=W_t[nm][c])
        for t in range(NT):
            for m in red_at[t]:
                E.wait(f"wsumd_{m}")
                E.dma("aw", f"redst_{m}",
                      out=red_d.ap().rearrange("o p -> p o"), in_=wsum[:])
                E.wait(f"redst_{m}")
                E.dma("aw", f"redld_{m}", out=row128[:], in_=red_d.ap())
                E.wait(f"cellr_{m}")
                E.dma("aw", f"cellst_{m}", out=cell_d.ap(), in_=tA[1][0:1, :])
                E.wait(f"cellst_{m}")
                E.dma("aw", f"swqld_{m}", out=swq[m][:],
                      in_=cell_d.ap().partition_broadcast(128))
            for k in p1_at[t]:
                m, src, dst, w = pieces[k]
                E.wait(stg_prev("1", k))
                E.dma("aw", f"wl1_{k}",
                      out=wstg[stg_buf("1", k)][:, :w], in_=src)
            for k in p2_at[t]:
                m, src, dst, w = pieces[k]
                E.wait(stg_prev("2", k))
                E.dma("aw", f"wl2_{k}",
                      out=wstg[stg_buf("2", k)][:, :w], in_=src)
                E.wait(f"wc2_{k}")
                E.dma("aw", f"ws2_{k}", out=dst, in_=wqo[wqo_buf(k)][:, :w])
            E.wait(f"A_evq{t}")
            E.dma("ag", f"A_sp{t}", out=q_t[t], in_=qtm[t % 2][:])
        E.wait("sacc_evac")
        E.dma("cc", "ar_in", out=cc_in.ap(), in_=sacc[:])

    def body1_gpsimd_end(E):
        pass

    def body1_vector(E):
        v = E.T
        E.inc(v.memset(wsum[:], 0.0), "AV", 1)
        for nm in ("Wq", "Wk0", "Wk1", "Wv0", "Wv1"):
            E.wait("ld_idb")
            for c in range(NF):
                E.wait(f"ldW_{nm}_{c}")
                d0 = C * c + 128 * c
                E.inc(v.tensor_tensor(W_sb[nm][:, d0:d0 + 128],
                                      W_sb[nm][:, d0:d0 + 128], idb_s[:],
                                      OP.add), "AV", 1)
        E.inc(v.engine_nop(), "AV", 1, "setup_v_done")

        def stats1(s):
            j = s % 2
            E.wait(f"A_x{s}")
            E.inc(v.reduce_sum(sm_s[:, s:s + 1], xbuf[s % 4][:], AX), "AV", 1)
            E.wait(f"A_sq{s}", None if s < 2 else f"A_sqd{s-2}")
            m, t1, mm, var, rve, msq = tA[0], tA[1], tA[2], tA[3], tA[4], tA[5]
            E.inc(v.tensor_scalar(m[:], sm_s[:, s:s + 1], 1.0 / C, None,
                                  OP.mult), "AV", 1)
            E.inc(v.tensor_scalar(t1[:], ssq_s[:, s:s + 1], 1.0 / C, None,
                                  OP.mult), "AV", 1)
            E.inc(v.tensor_tensor(mm[:], m[:], m[:], OP.mult), "AV", 1)
            E.inc(v.tensor_tensor(var[:], t1[:], mm[:], OP.subtract), "AV", 1)
            E.inc(v.tensor_scalar(duo[j][:, 0:1], var[:], 1e-5, None, OP.add),
                  "AV", 1)
            E.inc(v.reciprocal(rve[:], duo[j][:, 0:1]), "AV", 1)
            E.inc(v.tensor_tensor(msq[:], var[:], rve[:], OP.mult), "AV", 1)
            E.inc(v.tensor_scalar(duo[j][:, 1:2], msq[:], 1e-8, None, OP.add),
                  "AV", 1, f"A_duo{s}")
            E.inc(v.tensor_copy(tA[6][:], m[:]), "AV", 1)

        def stats2(s):
            j = s % 2
            E.wait(f"A_sqd{s}", None if s < 2 else f"A_tp{s-2}")
            rr, gg, mr, mrn = tA[0], tA[7], tA[2], tA[3]
            E.inc(v.reciprocal(rr[:], sduo[j][:, 0:1]), "AV", 1)
            E.inc(v.reciprocal(gg[:], sduo[j][:, 1:2]), "AV", 1)
            E.inc(v.tensor_tensor(rg_s[:, s:s + 1], rr[:], gg[:], OP.mult),
                  "AV", 1)
            E.inc(v.tensor_tensor(mr[:], tA[6][:], rr[:], OP.mult), "AV", 1)
            E.inc(v.tensor_scalar(mrn[:], mr[:], -1.0, None, OP.mult), "AV", 1)
            E.inc(v.tensor_scalar(mrgn_s[:, s:s + 1], mr[:], gg[:], -1.0,
                                  OP.mult, OP.mult), "AV", 1)
            E.inc(v.tensor_scalar(xnbuf[j][:], xbuf[s % 4][:], rr[:], mrn[:],
                                  OP.mult, OP.add), "AV", 1, f"A_xn{s}")

        stats1(0)
        stats2(0)
        for t in range(NT):
            if t + 1 < NT:
                stats1(t + 1)
            for m_ in red_at[t]:
                E.wait(f"redld_{m_}")
                E.inc(v.reduce_sum(tA[1][0:1, :], row128[:], AX), "AV", 1)
                E.inc(v.tensor_scalar(tA[1][0:1, :], tA[1][0:1, :],
                                      1.0 / NUMEL[m_], None, OP.mult), "AV", 1)
                E.inc(v.tensor_scalar(tA[1][0:1, :], tA[1][0:1, :], 1e-5,
                                      None, OP.max), "AV", 1)
                E.inc(v.reciprocal(tA[1][0:1, :], tA[1][0:1, :]), "AV", 1,
                      f"cellr_{m_}")
                E.wait(f"swqld_{m_}")
                E.inc(v.reciprocal(invsw[m_][:], swq[m_][:]), "AV", 1)
                E.inc(v.memset(wsum[:], 0.0), "AV", 1, f"wred_{m_}")
            for k in p1_at[t]:
                m_, src, dst, w = pieces[k]
                E.wait(f"wl1_{k}")
                E.inc(v.reduce_sum(tA[1][:], wstg[stg_buf("1", k)][:, :w], AX,
                                   apply_absolute_value=True), "AV", 1)
                E.inc(v.tensor_tensor(wsum[:], wsum[:], tA[1][:], OP.add),
                      "AV", 1, f"wc1_{k}")
                if k == NP1 - 1 or k == NP1 + 32 - 1:
                    E.inc(v.engine_nop(), "AV", 1,
                          f"wsumd_{0 if k < NP1 else 1}")
            for k in p2_at[t]:
                m_, src, dst, w = pieces[k]
                E.wait(f"wl2_{k}", f"wred_{m_}", wqo_prev(k))
                E.inc(v.tensor_scalar(wzf[:, :w], wstg[stg_buf("2", k)][:, :w],
                                      swq[m_][:], MAGIC, OP.mult, OP.add),
                      "AV", 1)
                E.inc(v.tensor_scalar(wzf[:, :w], wzf[:, :w], -MAGIC, None,
                                      OP.add), "AV", 1)
                E.inc(v.tensor_scalar(wqo[wqo_buf(k)][:, :w], wzf[:, :w], 1.0,
                                      -1.0, OP.min, OP.max), "AV", 1,
                      f"wc2_{k}")
            if t + 1 < NT:
                stats2(t + 1)
        E.wait(f"A_s1{NT-1}", f"A_s0{NT-1}")
        E.inc(v.tensor_copy(sacc[:, 0:1024], ps_states[:, 0:1024]), "AV", 1)
        E.inc(v.tensor_copy(sacc[:, 1024:2048], ps_states[:, 1024:2048]),
              "AV", 1, "sacc_evac")

    def body1_scalar(E):
        s = E.T
        E.wait(f"A_x0")
        E.inc(s.activation(xsq[:], xbuf[0][:], AF.Square,
                           accum_out=ssq_s[:, 0:1]), "AS", 1, "A_sq0")
        E.wait("A_duo0")
        E.inc(s.activation(sduo[0][:], duo[0][:], AF.Sqrt), "AS", 1, "A_sqd0")
        for t in range(NT + 1):
            if t + 1 < NT:
                tn = t + 1
                E.wait(f"A_x{tn}")
                E.inc(s.activation(xsq[:], xbuf[tn % 4][:], AF.Square,
                                   accum_out=ssq_s[:, tn:tn + 1]), "AS", 1,
                      f"A_sq{tn}")
            if t >= 1:
                tp = t - 1
                jp = tp % 2
                E.wait(f"A_q{tp}", None if tp < 2 else f"A_sp{tp-2}")
                E.inc(s.activation(qtm[jp][:], ps_rot[1][:], AF.Identity),
                      "AS", 1, f"A_evq{tp}")
                E.wait(f"A_k0{tp}", None if tp < 2 else f"A_s0{tp-2}")
                E.inc(s.activation(ktm[0][jp][:], ps_rot[0][:], AF.Identity),
                      "AS", 1, f"A_evk0{tp}")
                E.wait(f"A_v0{tp}", None if tp < 2 else f"A_s0{tp-2}")
                E.inc(s.activation(vtm[0][jp][:], ps_rot[1][:], AF.Identity),
                      "AS", 1, f"A_evv0{tp}")
                E.wait(f"A_k1{tp}", None if tp < 2 else f"A_s1{tp-2}")
                E.inc(s.activation(ktm[1][jp][:], ps_rot[0][:], AF.Identity),
                      "AS", 1, f"A_evk1{tp}")
                E.wait(f"A_v1{tp}", None if tp < 2 else f"A_s1{tp-2}")
                E.inc(s.activation(vtm[1][jp][:], ps_rot[1][:], AF.Identity),
                      "AS", 1, f"A_evv1{tp}")
            if t < NT:
                E.wait(f"A_tp{t}", None if t < 2 else f"A_v1{t-2}")
                E.inc(s.activation(xnfm[t % 2][:], ps_rot[0][:], AF.Identity),
                      "AS", 1, f"A_evtp{t}")
                if t + 1 < NT:
                    tn = t + 1
                    E.wait(f"A_duo{tn}", f"A_xn{tn-2}" if tn >= 2 else None)
                    E.inc(s.activation(sduo[tn % 2][:], duo[tn % 2][:],
                                       AF.Sqrt), "AS", 1, f"A_sqd{tn}")

    _BIAS = {"Wq": "bq", "Wk0": "bk0", "Wk1": "bk1", "Wv0": "bv0",
             "Wv1": "bv1"}

    def _proj(E, T, dst_ps, src_fm, Wname, t, ev):
        for f in range(NF):
            T.ldweights(src_fm[:, 128 * f:128 * (f + 1)])
            for h in range(2):
                T.matmul(dst_ps[:, 512 * h:512 * (h + 1)],
                         src_fm[:, 128 * f:128 * (f + 1)],
                         W_sb[Wname][:, C * f + 512 * h:C * f + 512 * (h + 1)],
                         start=(f == 0), stop=False)
        b = brow[_BIAS[Wname]]
        T.ldweights(ones_r[:])
        for h in range(2):
            ins = T.matmul(dst_ps[:, 512 * h:512 * (h + 1)], ones_r[:],
                           b[:, 512 * h:512 * (h + 1)],
                           start=False, stop=True)
        E.inc(ins, "AT", 1, ev)

    def body1_tensor(E):
        T = E.T
        E.wait("A_xn0", "setup_v_done", "ld_idf")
        ins = None
        for c in range(NF):
            ins = T.transpose(ps_rot[0][:, 128 * c:128 * (c + 1)],
                              xnbuf[0][:, 128 * c:128 * (c + 1)], idf_s[:])
        E.inc(ins, "AT", 1, "A_tp0")
        for u in range(1, NT + 1):
            t = u - 1
            j = t % 2
            E.wait(f"A_evtp{t}", None if t < 1 else f"A_evv1{t-1}")
            _proj(E, T, ps_rot[1], xnfm[j], "Wq", t, f"A_q{t}")
            _proj(E, T, ps_rot[0], xnfm[j], "Wk0", t, f"A_k0{t}")
            E.wait(f"A_evq{t}")
            _proj(E, T, ps_rot[1], xnfm[j], "Wv0", t, f"A_v0{t}")
            E.wait(f"A_evv0{t}")
            for p in range(8):
                T.ldweights(ktm[0][j][:, 128 * p:128 * (p + 1)])
                ins = T.matmul(ps_states[:, 128 * p:128 * (p + 1)],
                               ktm[0][j][:, 128 * p:128 * (p + 1)],
                               vtm[0][j][:, 128 * p:128 * (p + 1)],
                               start=(t == 0 and p % 4 == 0),
                               stop=(t == NT - 1), skip_group_check=True)
            E.inc(ins, "AT", 1, f"A_s0{t}")
            E.wait(f"A_evk0{t}")
            _proj(E, T, ps_rot[0], xnfm[j], "Wk1", t, f"A_k1{t}")
            _proj(E, T, ps_rot[1], xnfm[j], "Wv1", t, f"A_v1{t}")
            if t + 1 < NT:
                E.wait(f"A_xn{t+1}", f"A_evk1{t}")
                for c in range(NF):
                    ins = T.transpose(
                        ps_rot[0][:, 128 * c:128 * (c + 1)],
                        xnbuf[(t + 1) % 2][:, 128 * c:128 * (c + 1)], idf_s[:])
                E.inc(ins, "AT", 1, f"A_tp{t+1}")
            E.wait(f"A_evv1{t}")
            for p in range(8):
                T.ldweights(ktm[1][j][:, 128 * p:128 * (p + 1)])
                ins = T.matmul(ps_states[:, 1024 + 128 * p:1024 + 128 * (p + 1)],
                               ktm[1][j][:, 128 * p:128 * (p + 1)],
                               vtm[1][j][:, 128 * p:128 * (p + 1)],
                               start=(t == 0 and p % 4 == 0),
                               stop=(t == NT - 1), skip_group_check=True)
            E.inc(ins, "AT", 1, f"A_s1{t}")

    body1 = [("sync", body1_sync), ("gpsimd", body1_gpsimd),
             ("vector", body1_vector), ("scalar", body1_scalar),
             ("tensor", body1_tensor)]

    A_end = [f"A_s1{NT-1}", "sacc_evac", f"A_sqd{NT-1}",
             f"A_sp{NT-1}", f"ws2_{NP1 + 32 - 1}", f"A_x{NT-1}"]

    es_b = ExitStack()
    Bt = {}

    def alloc_b():
        sbb = lambda n, s, d: es_b.enter_context(nc.sbuf_tensor(n, s, d))
        Bt["w1q"] = sbb("w1q_sb", [128, NF * HID], BF16)
        Bt["w2q"] = sbb("w2q_sb", [128, NH2 * C], BF16)
        Bt["n1w"] = sbb("n1w_rep", [128, C], F32)
        Bt["n2w"] = sbb("n2w_rep", [128, HID], BF16)
        Bt["hact"] = sbb("hact", [128, HID], BF16)
        Bt["u"] = sbb("u_buf", [128, 1024], F32)
        Bt["xh"] = sbb("xh", [128, C], F32)
        Bt["xq"] = sbb("xq", [128, C], BF16)
        Bt["xqfm"] = sbb("xqfm", [128, C], BF16)
        Bt["xq2tm"] = [sbb(f"xq2tm{j}", [128, HID], BF16) for j in range(2)]
        Bt["xq2fm"] = sbb("xq2fm", [128, HID], BF16)
        Bt["qs1"] = [sbb(f"qs1_{j}", [128, 1], F32) for j in range(2)]
        Bt["beta"] = [sbb(f"beta{j}", [128, 1], F32) for j in range(2)]
        Bt["alpha"] = [sbb(f"alpha{j}", [128, 1], F32) for j in range(3)]
        Bt["qs2"] = sbb("qs2", [128, 1], F32)
        Bt["sqacc"] = sbb("sqacc", [128, 4], F32)
        Bt["geB"] = [sbb(f"geB{j}", [128, 1], F32) for j in range(2)]
        Bt["sgeB"] = [sbb(f"sgeB{j}", [128, 1], F32) for j in range(2)]
        Bt["tB"] = [sbb(f"tB{j}", [128, 1], F32) for j in range(4)]
        Bt["ps_h"] = [es_b.enter_context(
            nc.psum_tensor(f"ps_h{j}", [128, 1024], F32)) for j in range(2)]
        Bt["ps_tpB"] = [es_b.enter_context(
            nc.psum_tensor(f"ps_tpB{j}", [128, 1024], BF16)) for j in range(2)]
        Bt["ps_y2"] = es_b.enter_context(
            nc.psum_tensor("ps_y2", [128, 1024], F32))

    def body2_sync(E):
        s = E.T
        E.wait(*A_end)
        for f in range(NF):
            E.dma("bw", f"B_w1l{f}",
                  out=Bt["w1q"][:, HID * f:HID * (f + 1)], in_=w1q_t[f])
        for h in range(NH2):
            E.dma("bw", f"B_w2l{h}",
                  out=Bt["w2q"][:, C * h:C * (h + 1)], in_=w2q_t[h])
        for t in range(NT):
            if t >= 4:
                E.wait(f"B_yp{t-4}")
            E.dma("bx", f"B_x{t}", out=xbuf[t % 4][:], in_=x_t[t])

    def body2_gpsimd(E):
        g = E.T
        E.wait(*A_end)
        E.dma("cc", "B_n1w", out=Bt["n1w"][:],
              in_=vrow["n1w"].ap().partition_broadcast(128))
        E.dma("cc", "B_n2w", out=Bt["n2w"][:],
              in_=n2w_h.ap().partition_broadcast(128))
        E.wait("ar_in")
        E.inc(g.collective_compute(
            "AllReduce", OP.add,
            replica_groups=[[0, 1], [2, 3], [4, 5], [6, 7]],
            ins=[cc_in.ap()], outs=[cc_out.ap()]), "CCS", 1, "ar_cc")

    def _b_xq_step(E, v, tn):
        jn = tn % 2
        E.wait(f"B_x{tn}", "B_n1w",
               None if tn < 2 else f"B_relu{tn-2}_3",
               None if tn < 4 else f"B_ydma{tn-4}",
               None if tn < 1 else f"B_tpq{tn-1}")
        E.inc(v.tensor_scalar(Bt["xh"][:], xbuf[tn % 4][:], rg_s[:, tn:tn + 1],
                              mrgn_s[:, tn:tn + 1], OP.mult, OP.add), "BV", 1)
        E.inc(v.tensor_tensor(Bt["xh"][:], Bt["xh"][:], Bt["n1w"][:],
                              OP.mult), "BV", 1)
        E.inc(v.reduce_max(Bt["tB"][0][:], Bt["xh"][:], AX,
                           apply_absolute_value=True), "BV", 1)
        E.inc(v.tensor_scalar(Bt["tB"][0][:], Bt["tB"][0][:], 1e-5, None,
                              OP.max), "BV", 1)
        E.inc(v.reciprocal(Bt["tB"][1][:], Bt["tB"][0][:]), "BV", 1)
        E.inc(v.tensor_scalar(Bt["qs1"][jn][:], Bt["tB"][1][:], 127.0, None,
                              OP.mult), "BV", 1)
        E.inc(v.reciprocal(Bt["tB"][2][:], Bt["qs1"][jn][:]), "BV", 1)
        E.inc(v.tensor_tensor(Bt["beta"][jn][:], Bt["tB"][2][:], invsw[0][:],
                              OP.mult), "BV", 1)
        E.inc(v.tensor_scalar(Bt["xh"][:], Bt["xh"][:], Bt["qs1"][jn][:],
                              MAGIC, OP.mult, OP.add), "BV", 1)
        E.inc(v.tensor_scalar(Bt["xq"][:], Bt["xh"][:], -MAGIC, None,
                              OP.add), "BV", 1, f"B_xq{tn}")

    def body2_vector(E):
        v = E.T
        E.wait(*A_end)
        _b_xq_step(E, v, 0)
        for t in range(NT + 2):
            j = t % 2
            if t < NT:
                E.wait(f"B_tpq{t}")
                E.inc(v.tensor_copy(Bt["xqfm"][:], Bt["ps_tpB"][0][:]), "BV",
                      1, f"B_xqfm{t}")
            if t >= 2 and t - 2 < NT:
                tr = t - 2
                for q in range(4):
                    E.wait(f"B_tpxq2_{tr}_{q}")
                    E.inc(v.tensor_copy(
                        Bt["xq2fm"][:, 1024 * q:1024 * (q + 1)],
                        Bt["ps_tpB"][(q + 1) % 2][:]), "BV", 1,
                        f"B_xq2fm{tr}_{q}")
            if t + 1 < NT:
                _b_xq_step(E, v, t + 1)
            if t < NT:
                E.wait(f"B_sqacc{t}_3")
                E.inc(v.tensor_tensor(Bt["tB"][0][:], Bt["sqacc"][:, 0:1],
                                      Bt["sqacc"][:, 1:2], OP.add), "BV", 1)
                E.inc(v.tensor_tensor(Bt["tB"][1][:], Bt["sqacc"][:, 2:3],
                                      Bt["sqacc"][:, 3:4], OP.add), "BV", 1)
                E.inc(v.tensor_tensor(Bt["tB"][0][:], Bt["tB"][0][:],
                                      Bt["tB"][1][:], OP.add), "BV", 1)
                E.inc(v.tensor_scalar(Bt["geB"][j][:], Bt["tB"][0][:],
                                      1.0 / HID, 1e-8, OP.mult, OP.add),
                      "BV", 1, f"B_ge{t}")
                E.wait(f"B_sge{t}", "B_n2w")
                E.inc(v.reciprocal(Bt["tB"][2][:], Bt["sgeB"][j][:]), "BV", 1)
                for q in range(4):
                    E.inc(v.scalar_tensor_tensor(
                        Bt["hact"][:, 1024 * q:1024 * (q + 1)],
                        Bt["hact"][:, 1024 * q:1024 * (q + 1)], Bt["tB"][2][:],
                        Bt["n2w"][:, 1024 * q:1024 * (q + 1)], OP.mult,
                        OP.mult), "BV", 1)
                E.inc(v.reduce_max(Bt["tB"][0][:], Bt["hact"][:], AX,
                                   apply_absolute_value=True), "BV", 1)
                E.inc(v.tensor_scalar(Bt["tB"][0][:], Bt["tB"][0][:], 1e-5,
                                      None, OP.max), "BV", 1)
                E.inc(v.reciprocal(Bt["tB"][1][:], Bt["tB"][0][:]), "BV", 1)
                E.inc(v.tensor_scalar(Bt["qs2"][:], Bt["tB"][1][:], 127.0,
                                      None, OP.mult), "BV", 1)
                E.inc(v.reciprocal(Bt["tB"][3][:], Bt["qs2"][:]), "BV", 1)
                E.inc(v.tensor_tensor(Bt["alpha"][t % 3][:], Bt["tB"][3][:],
                                      invsw[1][:], OP.mult), "BV", 1)
                for q in range(4):
                    E.inc(v.tensor_scalar(Bt["u"][:],
                                          Bt["hact"][:, 1024 * q:1024 * (q + 1)],
                                          Bt["qs2"][:], MAGIC, OP.mult,
                                          OP.add), "BV", 1)
                    E.inc(v.tensor_scalar(
                        Bt["xq2tm"][t % 2][:, 1024 * q:1024 * (q + 1)],
                        Bt["u"][:], -MAGIC, None, OP.add), "BV", 1,
                        f"B_xq2_{t}" if q == 3 else None)
            if t >= 2 and t - 2 < NT:
                tr = t - 2
                E.wait(f"B_w2_{tr}", None if tr < 1 else f"B_ydma{tr-1}")
                E.inc(v.scalar_tensor_tensor(
                    Bt["xh"][:], Bt["ps_y2"][:], Bt["alpha"][tr % 3][:],
                    xbuf[tr % 4][:], OP.mult, OP.add), "BV", 1, f"B_yp{tr}")

    def body2_scalar(E):
        s = E.T
        E.wait(*A_end)
        for t in range(NT + 2):
            j = t % 2
            if t < NT:
                for q in range(4):
                    E.wait(f"B_w1_{t}_{q}",
                           f"B_xq2_{t-1}" if (q == 0 and t >= 1) else None)
                    E.inc(s.activation(Bt["u"][:], Bt["ps_h"][q % 2][:],
                                       AF.Relu, scale=Bt["beta"][j][:]),
                          "BS", 1, f"B_relu{t}_{q}")
                    E.wait(None if t < 1 else f"B_xq2_{t-1}")
                    E.inc(s.activation(Bt["hact"][:, 1024 * q:1024 * (q + 1)],
                                       Bt["u"][:], AF.Square), "BS", 1)
                    E.inc(s.activation(Bt["hact"][:, 1024 * q:1024 * (q + 1)],
                                       Bt["hact"][:, 1024 * q:1024 * (q + 1)],
                                       AF.Identity, bias=b_pp[:],
                                       scale=s_pp[:]), "BS", 1)
                    E.inc(s.activation(Bt["u"][:],
                                       Bt["hact"][:, 1024 * q:1024 * (q + 1)],
                                       AF.Square,
                                       accum_out=Bt["sqacc"][:, q:q + 1]),
                          "BS", 1, f"B_sqacc{t}_{q}")
                E.wait(f"B_ge{t}")
                E.inc(s.activation(Bt["sgeB"][j][:], Bt["geB"][j][:], AF.Sqrt),
                      "BS", 1, f"B_sge{t}")
            if t >= 2 and t - 2 < NT:
                tr = t - 2
                E.wait(f"B_yp{tr}")
                E.dma("by", f"B_ydma{tr}", out=yp_t[tr], in_=Bt["xh"][:])

    def _b_w2(E, T, t):
        E.wait(f"B_xq2fm{t}_3", None if t < 1 else f"B_yp{t-1}",
               f"B_w2l{NH2-1}" if t == 0 else None)
        for h in range(NH2):
            T.ldweights(Bt["xq2fm"][:, 128 * h:128 * (h + 1)])
            for g2 in range(2):
                ins = T.matmul(
                    Bt["ps_y2"][:, 512 * g2:512 * (g2 + 1)],
                    Bt["xq2fm"][:, 128 * h:128 * (h + 1)],
                    Bt["w2q"][:, C * h + 512 * g2:C * h + 512 * (g2 + 1)],
                    start=(h == 0), stop=(h == NH2 - 1))
        E.inc(ins, "BT", 1, f"B_w2_{t}")

    def body2_tensor(E):
        T = E.T
        E.wait(*A_end, f"B_w1l{NF-1}")
        for t in range(NT + 2):
            if t < NT:
                E.wait(f"B_xq{t}",
                       None if t < 3 else f"B_xq2fm{t-3}_3")
                for c in range(NF):
                    ins = T.transpose(Bt["ps_tpB"][0][:, 128 * c:128 * (c + 1)],
                                      Bt["xq"][:, 128 * c:128 * (c + 1)],
                                      idb_s[:])
                E.inc(ins, "BT", 1, f"B_tpq{t}")
                for q in range(4):
                    E.wait(f"B_xqfm{t}" if q == 0 else None,
                           (f"B_relu{t}_{q-2}" if q >= 2 else
                            (f"B_relu{t-1}_{q+2}" if t >= 1 else None)))
                    for f in range(NF):
                        T.ldweights(Bt["xqfm"][:, 128 * f:128 * (f + 1)])
                        for g2 in range(2):
                            ins = T.matmul(
                                Bt["ps_h"][q % 2][:, 512 * g2:512 * (g2 + 1)],
                                Bt["xqfm"][:, 128 * f:128 * (f + 1)],
                                Bt["w1q"][:, HID * f + 1024 * q + 512 * g2:
                                           HID * f + 1024 * q + 512 * (g2 + 1)],
                                start=(f == 0), stop=(f == NF - 1))
                    E.inc(ins, "BT", 1, f"B_w1_{t}_{q}")
            if t >= 2 and t - 2 < NT:
                tr = t - 2
                for q in range(4):
                    bk = (q + 1) % 2
                    if q == 0:
                        E.wait(f"B_xq2_{tr}",
                               None if tr < 1 else f"B_xq2fm{tr-1}_2")
                    elif q == 1:
                        E.wait(f"B_xqfm{t}" if t < NT
                               else f"B_xq2fm{tr-1}_3")
                    else:
                        E.wait(f"B_xq2fm{tr}_{q-2}")
                    for c in range(8):
                        ins = T.transpose(
                            Bt["ps_tpB"][bk][:, 128 * c:128 * (c + 1)],
                            Bt["xq2tm"][tr % 2][:, 1024 * q + 128 * c:
                                                1024 * q + 128 * (c + 1)],
                            idb_s[:])
                    E.inc(ins, "BT", 1, f"B_tpxq2_{tr}_{q}")
                _b_w2(E, T, tr)

    body2 = [("sync", body2_sync), ("gpsimd", body2_gpsimd),
             ("vector", body2_vector), ("scalar", body2_scalar),
             ("tensor", body2_tensor)]

    B_end = [f"B_w2_{NT-1}", f"B_yp{NT-1}", f"B_ydma{NT-1}",
             f"B_sge{NT-1}", f"B_x{NT-1}"]

    es_c = ExitStack()
    Ct = {}

    def alloc_c():
        sbc = lambda n, s, d: es_c.enter_context(nc.sbuf_tensor(n, s, d))
        Ct["Wo"] = sbc("Wo_sb", [128, NF * C], BF16)
        Ct["lng"] = sbc("lng_rep", [128, C], F32)
        Ct["lnb"] = sbc("lnb_rep", [128, C], F32)
        Ct["bo"] = sbc("bo_rep", [128, C], F32)
        Ct["qb"] = [sbc(f"qbC{j}", [128, C], BF16) for j in range(2)]
        Ct["ypb"] = [sbc(f"ypbC{j}", [128, C], F32) for j in range(3)]
        Ct["qfm"] = sbc("qfmC", [128, C], BF16)
        Ct["ot"] = [sbc(f"otC{j}", [128, C], F32) for j in range(3)]
        Ct["m2"] = [sbc(f"m2C{j}", [128, 1], F32) for j in range(2)]
        Ct["osc"] = sbc("oscC", [128, C], F32)
        Ct["oln"] = [sbc(f"olnC{j}", [128, C], BF16) for j in range(2)]
        Ct["olnfm"] = sbc("olnfmC", [128, C], BF16)
        Ct["yb"] = [sbc(f"ybC{j}", [128, C], F32) for j in range(2)]
        Ct["so"] = sbc("soC", [128, 1], F32)
        Ct["so2"] = [sbc(f"so2C{j}", [128, 1], F32) for j in range(2)]
        Ct["ve"] = [sbc(f"veC{j}", [128, 1], F32) for j in range(2)]
        Ct["sve"] = [sbc(f"sveC{j}", [128, 1], F32) for j in range(2)]
        Ct["tC"] = [sbc(f"tC{j}", [128, 1], F32) for j in range(4)]
        Ct["sres"] = sbc("sresC", [128, 2048], F32)
        Ct["state"] = sbc("state_sb", [128, C], BF16)
        Ct["ps_tp"] = [es_c.enter_context(
            nc.psum_tensor(f"ps_tpC{j}", [128, C], BF16)) for j in range(2)]
        Ct["ps_out"] = es_c.enter_context(
            nc.psum_tensor("ps_outC", [128, C], F32))
        Ct["ps_yw"] = es_c.enter_context(
            nc.psum_tensor("ps_ywC", [128, C], F32))

    def body3_sync(E):
        s = E.T
        E.wait(*B_end)
        for t in range(NT):
            if t >= 2:
                E.wait(f"C_qfm{t-2}")
            E.dma("cx", f"C_ql{t}", out=Ct["qb"][t % 2][:], in_=q_t[t])
            if t >= 3:
                E.wait(f"C_y{t-3}")
            E.dma("cx", f"C_ypl{t}", out=Ct["ypb"][t % 3][:], in_=yp_t[t])
        E.wait(f"C_ydma{NT-1}", f"C_ydma{NT-2}")

    def body3_gpsimd(E):
        g = E.T
        E.wait(*B_end, "ar_cc")
        E.dma("cc", "ar_done", out=Ct["sres"][:], in_=cc_out.ap())
        for c in range(NF):
            E.dma("aw", f"C_Wo{c}", out=Ct["Wo"][:, C * c:C * (c + 1)],
                  in_=W_t["Wo"][c])
        for nm in ("lng", "lnb", "bo"):
            E.dma("aw", f"C_{nm}", out=Ct[nm][:],
                  in_=vrow[nm].ap().partition_broadcast(128))
        for t in range(NT):
            E.wait(f"C_y{t}")
            E.dma("cy", f"C_ydma{t}", out=y_t[t], in_=Ct["yb"][t % 2][:])

    def body3_vector(E):
        v = E.T
        E.wait(*B_end, "C_lng", "C_lnb", "C_bo", "ar_done")
        E.inc(v.tensor_tensor(Ct["sres"][:, 0:1024], Ct["sres"][:, 0:1024],
                              Ct["sres"][:, 1024:2048], OP.mult), "CV", 1)
        E.inc(v.tensor_scalar(Ct["state"][:], Ct["sres"][:, 0:1024],
                              1.0 / 16777216.0, None, OP.mult), "CV", 1)
        ins = None
        for p in range(8):
            v.memset(Ct["state"][0:64, 128 * p + 64:128 * (p + 1)], 0.0)
            ins = v.memset(Ct["state"][64:128, 128 * p:128 * p + 64], 0.0)
        E.inc(ins, "CV", 1, "state_ready")
        for t in range(NT + 3):
            if t >= 1 and t - 1 < NT:
                s_ = t - 1
                j = s_ % 2
                E.wait(f"C_ot{s_}")
                E.inc(v.reduce_sum(Ct["so"][:], Ct["ot"][s_ % 3][:], AX),
                      "CV", 1)
                E.wait(f"C_sq{s_}")
                m2, vv_, mm2 = Ct["m2"][j], Ct["tC"][1], Ct["tC"][2]
                E.inc(v.tensor_scalar(m2[:], Ct["so"][:], 1.0 / C, None,
                                      OP.mult), "CV", 1)
                E.inc(v.tensor_scalar(vv_[:], Ct["so2"][j][:], 1.0 / C, None,
                                      OP.mult), "CV", 1)
                E.inc(v.tensor_tensor(mm2[:], m2[:], m2[:], OP.mult), "CV", 1)
                E.inc(v.tensor_tensor(vv_[:], vv_[:], mm2[:], OP.subtract),
                      "CV", 1)
                E.inc(v.tensor_scalar(Ct["ve"][j][:], vv_[:], 1e-5, None,
                                      OP.add), "CV", 1, f"C_ve{s_}")
            if t >= 2 and t - 2 < NT:
                s2 = t - 2
                j2 = s2 % 2
                ob = Ct["ot"][s2 % 3]
                E.wait(f"C_sve{s2}", None if s2 < 2 else f"C_tpln{s2-2}")
                rstd, mn = Ct["tC"][1], Ct["tC"][2]
                E.inc(v.reciprocal(rstd[:], Ct["sve"][j2][:]), "CV", 1)
                E.inc(v.tensor_scalar(mn[:], Ct["m2"][j2][:], rstd[:], -1.0,
                                      OP.mult, OP.mult), "CV", 1)
                E.inc(v.tensor_scalar(ob[:], ob[:], rstd[:],
                                      mn[:], OP.mult, OP.add), "CV", 1)
                E.inc(v.tensor_tensor(ob[:], ob[:],
                                      Ct["lng"][:], OP.mult), "CV", 1)
                E.inc(v.tensor_tensor(Ct["oln"][j2][:], ob[:],
                                      Ct["lnb"][:], OP.add), "CV", 1,
                      f"C_oln{s2}")
            if t >= 3:
                tr = t - 3
                E.wait(f"C_wo{tr}", f"C_ypl{tr}",
                       None if tr < 2 else f"C_ydma{tr-2}")
                jp = tr % 2
                E.inc(v.tensor_tensor(Ct["yb"][jp][:], Ct["ps_yw"][:],
                                      Ct["ypb"][tr % 3][:], OP.add), "CV", 1)
                E.inc(v.tensor_tensor(Ct["yb"][jp][:], Ct["yb"][jp][:],
                                      Ct["bo"][:], OP.add), "CV", 1,
                      f"C_y{tr}")

    def body3_scalar(E):
        s = E.T
        E.wait(*B_end)
        for t in range(NT + 3):
            if t < NT:
                E.wait(f"C_tpq{t}")
                E.inc(s.activation(Ct["qfm"][:], Ct["ps_tp"][0][:],
                                   AF.Identity), "CS", 1, f"C_qfm{t}")
                j3 = t % 3
                E.wait(f"C_out{t}", None if t < 3 else f"C_oln{t-3}")
                E.inc(s.activation(Ct["ot"][j3][:], Ct["ps_out"][:],
                                   AF.Identity), "CS", 1, f"C_ot{t}")
                E.wait(None if t < 2 else f"C_ve{t-2}")
                E.inc(s.activation(Ct["osc"][:], Ct["ot"][j3][:], AF.Square,
                                   accum_out=Ct["so2"][t % 2][:]), "CS", 1,
                      f"C_sq{t}")
            if t >= 1 and t - 1 < NT:
                s_ = t - 1
                E.wait(f"C_ve{s_}")
                E.inc(s.activation(Ct["sve"][s_ % 2][:], Ct["ve"][s_ % 2][:],
                                   AF.Sqrt), "CS", 1, f"C_sve{s_}")
            if t >= 3 and t - 3 < NT:
                tr = t - 3
                E.wait(f"C_tpln{tr}")
                E.inc(s.activation(Ct["olnfm"][:], Ct["ps_tp"][1][:],
                                   AF.Identity), "CS", 1, f"C_olnfm{tr}")

    def body3_tensor(E):
        T = E.T
        E.wait(*B_end, f"C_Wo{NF-1}")
        for t in range(NT + 3):
            if t < NT:
                E.wait(f"C_ql{t}", None if t < 1 else f"C_qfm{t-1}")
                for c in range(NF):
                    ins = T.transpose(Ct["ps_tp"][0][:, 128 * c:128 * (c + 1)],
                                      Ct["qb"][t % 2][:, 128 * c:128 * (c + 1)],
                                      idb_s[:])
                E.inc(ins, "CT", 1, f"C_tpq{t}")
                E.wait(f"C_qfm{t}", "state_ready",
                       None if t < 1 else f"C_ot{t-1}")
                for p in range(8):
                    T.ldweights(Ct["qfm"][:, 128 * p:128 * (p + 1)])
                    ins = T.matmul(Ct["ps_out"][:, 128 * p:128 * (p + 1)],
                                   Ct["qfm"][:, 128 * p:128 * (p + 1)],
                                   Ct["state"][:, 128 * p:128 * (p + 1)],
                                   start=True, stop=True)
                E.inc(ins, "CT", 1, f"C_out{t}")
            if t >= 3:
                tr = t - 3
                E.wait(f"C_oln{tr}", None if tr < 1 else f"C_olnfm{tr-1}")
                for c in range(NF):
                    ins = T.transpose(Ct["ps_tp"][1][:, 128 * c:128 * (c + 1)],
                                      Ct["oln"][tr % 2][:, 128 * c:128 * (c + 1)],
                                      idb_s[:])
                E.inc(ins, "CT", 1, f"C_tpln{tr}")
                E.wait(f"C_olnfm{tr}", None if tr < 1 else f"C_y{tr-1}")
                for c in range(NF):
                    T.ldweights(Ct["olnfm"][:, 128 * c:128 * (c + 1)])
                    for h in range(2):
                        ins = T.matmul(
                            Ct["ps_yw"][:, 512 * h:512 * (h + 1)],
                            Ct["olnfm"][:, 128 * c:128 * (c + 1)],
                            Ct["Wo"][:, C * c + 512 * h:C * c + 512 * (h + 1)],
                            start=(c == 0), stop=(c == NF - 1))
                E.inc(ins, "CT", 1, f"C_wo{tr}")

    body3 = [("sync", body3_sync), ("gpsimd", body3_gpsimd),
             ("vector", body3_vector), ("scalar", body3_scalar),
             ("tensor", body3_tensor)]

    class _Slc:
        def __getitem__(self, k):
            return self

    for key in ("w1q", "w2q", "n1w", "n2w", "hact", "u", "xh", "xq",
                "xqfm", "xq2tm", "xq2fm", "qs1", "beta", "alpha",
                "qs2", "sqacc", "geB", "sgeB", "tB", "ps_h", "ps_tpB",
                "ps_y2"):
        Bt[key] = _Slc()
    for key in ("Wo", "lng", "lnb", "bo", "qb", "ypb", "qfm", "ot", "osc",
                "oln", "olnfm", "yb", "so", "so2", "ve", "sve", "tC", "m2",
                "sres", "state", "ps_tp", "ps_out", "ps_yw"):
        Ct[key] = _Slc()
    gseq_plan = {}
    for nm, fn in body1 + body2 + body3:
        fn(_EB(sched, sems, counts, None, gseq_plan,
               serial=nm in ("vector", "scalar")))
    Bt.clear()
    Ct.clear()

    gseq_emit = {}
    with nc.Block("blk1") as blk:
        for nm, fn in body1:
            getattr(blk, nm)(lambda eng, fn=fn, nm=nm: fn(
                _EB(sched, sems, None, eng, gseq_emit,
                    serial=nm in ("vector", "scalar"))))
    es_a.close()
    alloc_b()
    with nc.Block("blk2") as blk:
        for nm, fn in body2:
            getattr(blk, nm)(lambda eng, fn=fn, nm=nm: fn(
                _EB(sched, sems, None, eng, gseq_emit,
                    serial=nm in ("vector", "scalar"))))
    es_b.close()
    alloc_c()
    with nc.Block("blk3") as blk:
        for nm, fn in body3:
            getattr(blk, nm)(lambda eng, fn=fn, nm=nm: fn(
                _EB(sched, sems, None, eng, gseq_emit,
                    serial=nm in ("vector", "scalar"))))
    es_c.close()
    es.close()
    return nc


def kernel(L0_attn_x, L0_attn_Wq, L0_attn_bq, L0_attn_Wk, L0_attn_bk,
           L0_attn_Wv, L0_attn_bv, L0_attn_Wo, L0_attn_bo, L0_attn_ln_g,
           L0_attn_ln_b, L0_ffn_norm1_w, L0_ffn_w1, L0_ffn_act_scale,
           L0_ffn_act_bias, L0_ffn_norm2_w, L0_ffn_w2):
    NT = 32
    TPC = NT * 128
    x = np.ascontiguousarray(np.asarray(L0_attn_x, dtype=np.float32)
                             ).reshape(8 * TPC, C)
    f32 = lambda a: np.ascontiguousarray(np.asarray(a, dtype=np.float32))
    row = lambda a: f32(a).reshape(1, -1)
    common = {
        "Wq": f32(L0_attn_Wq), "Wo": f32(L0_attn_Wo),
        "Wk0": f32(np.asarray(L0_attn_Wk)[0]),
        "Wk1": f32(np.asarray(L0_attn_Wk)[1]),
        "Wv0": f32(np.asarray(L0_attn_Wv)[0]),
        "Wv1": f32(np.asarray(L0_attn_Wv)[1]),
        "w1t": f32(np.asarray(L0_ffn_w1).T),
        "w2t": f32(np.asarray(L0_ffn_w2).T),
        "bq": row(L0_attn_bq), "bk0": row(np.asarray(L0_attn_bk)[0]),
        "bk1": row(np.asarray(L0_attn_bk)[1]),
        "bv0": row(np.asarray(L0_attn_bv)[0]),
        "bv1": row(np.asarray(L0_attn_bv)[1]), "bo": row(L0_attn_bo),
        "lng": row(L0_attn_ln_g), "lnb": row(L0_attn_ln_b),
        "n1w": row(L0_ffn_norm1_w), "n2w": row(L0_ffn_norm2_w),
        "acts": f32(L0_ffn_act_scale).reshape(1, 1),
        "actb": f32(L0_ffn_act_bias).reshape(1, 1),
    }
    in_maps = [dict(common, x=np.ascontiguousarray(x[c * TPC:(c + 1) * TPC]))
               for c in range(8)]
    nc = build_nc(NT)
    res = run_bass_kernel_spmd(nc, in_maps, core_ids=list(range(8)))
    y = np.concatenate([np.asarray(res.results[c]["y"]) for c in range(8)],
                       axis=0)
    return y.reshape(4, 8192, C)


# version 11 (final)
# speedup vs baseline: 1.1943x; 1.1293x over previous
import numpy as np
import concourse.bass as bass
import concourse.mybir as mybir
from concourse.bass_utils import run_bass_kernel_spmd
from contextlib import ExitStack
import ml_dtypes

F32 = mybir.dt.float32
BF16 = mybir.dt.bfloat16
OP = mybir.AluOpType
AF = mybir.ActivationFunctionType
AX = mybir.AxisListType.X
MAGIC = 12582912.0
C = 1024
HID = 4096
NF = C // 128
NH2 = HID // 128


class _Dummy:
    def __getattr__(self, name):
        return lambda *a, **k: None


GROUP_SEMS = {
    "ax": ["AX0", "AX1"],
    "aw": ["AW0", "AW1", "AW2", "AW3"],
    "ag": ["AG0", "AG1"],
    "bx": ["BX0", "BX1"],
    "bw": ["BW0", "BW1", "BW2", "BW3"],
    "by": ["BY0", "BY1"],
    "cc": ["CC0", "CC1"],
    "cx": ["CX0", "CX1", "CX2", "CX3"],
    "cy": ["CY0", "CY1"],
    "cz": ["CZ0", "CZ1"],
}


class _EB:

    def __init__(self, sched, sems, counts, eng, gseq, serial=False):
        self.s = sched
        self.sems = sems
        self.counts = counts
        self.eng = eng
        self.gseq = gseq
        self.serial = serial
        self.ecnt = {}
        self.T = eng if eng is not None else _Dummy()

    def dma(self, group, name=None, **kw):
        seq = self.gseq.get(group, 0)
        self.gseq[group] = seq + 1
        sl = GROUP_SEMS[group]
        s = sl[seq % len(sl)]
        auto = f"@{group}_{seq}"
        if self.eng is None:
            c = self.counts
            c[s] = c.get(s, 0) + 16
            assert auto not in self.s
            self.s[auto] = (s, c[s])
            if name is not None:
                assert name not in self.s, f"duplicate event {name}"
                self.s[name] = (s, c[s])
        else:
            if seq >= len(sl):
                self.wait(f"@{group}_{seq - len(sl)}")
            ins = self.eng.dma_start(**kw)
            ins.then_inc(self.sems[s], 16)

    def wait(self, *names):
        if self.eng is None:
            return
        for n in names:
            if n is None:
                continue
            sem, val = self.s[n]
            self.eng.wait_ge(self.sems[sem], val)

    def inc(self, ins, sem, amt, name=None):
        if self.eng is None:
            c = self.counts
            c[sem] = c.get(sem, 0) + amt
            if name is not None:
                assert name not in self.s, f"duplicate event {name}"
                self.s[name] = (sem, c[sem])
            if self.serial:
                self.ecnt[sem] = c[sem]
        else:
            if ins is None:
                raise RuntimeError(f"missing instruction for inc {name}")
            ins.then_inc(self.sems[sem], amt)
            if self.serial:
                self.ecnt[sem] = self.ecnt.get(sem, 0) + amt
                self.eng.wait_ge(self.sems[sem], self.ecnt[sem])


def build_nc(NT):
    TPC = NT * 128
    nc = bass.Bass("TRN2", target_bir_lowering=False,
                   detect_race_conditions=False)

    din = lambda n, s: nc.dram_tensor(n, s, F32, kind="ExternalInput")
    x_h = din("x", [TPC, C])
    W_h = {nm: din(nm, [C, C]) for nm in ("Wq", "Wk0", "Wk1", "Wv0", "Wv1", "Wo")}
    w1t_h = din("w1t", [C, HID])
    w2t_h = din("w2t", [HID, C])
    vrow = {nm: din(nm, [1, C]) for nm in ("bq", "bk0", "bk1", "bv0", "bv1",
                                          "bo", "lng", "lnb", "n1w")}
    n2w_h = din("n2w", [1, HID])
    acts_h = din("acts", [1, 1])
    actb_h = din("actb", [1, 1])
    y_h = nc.dram_tensor("y", [TPC, C], F32, kind="ExternalOutput")

    q_sp = nc.dram_tensor("q_sp", [TPC, C], BF16, kind="Internal")
    yp_sp = nc.dram_tensor("yp_sp", [TPC, C], F32, kind="Internal")
    w1q_sp = nc.dram_tensor("w1q_sp", [C, HID], BF16, kind="Internal")
    w2q_sp = nc.dram_tensor("w2q_sp", [HID, C], BF16, kind="Internal")
    cc_in = nc.dram_tensor("cc_in", [128, 2048], F32, kind="Internal")
    cc_out = nc.dram_tensor("cc_out", [128, 2048], F32, kind="Internal")
    red_d = nc.dram_tensor("red_d", [1, 128], F32, kind="Internal")
    cell_d = nc.dram_tensor("cell_d", [1, 1], F32, kind="Internal")

    idf_d = nc.inline_tensor(np.eye(128, dtype=np.float32), "idf_d")
    idb_d = nc.inline_tensor(np.eye(128, dtype=ml_dtypes.bfloat16), "idb_d")
    ones_d = nc.inline_tensor(np.ones((1, 128), dtype=ml_dtypes.bfloat16),
                              "ones_d")

    x_t = x_h.ap().rearrange("(n p) c -> n p c", p=128)
    y_t = y_h.ap().rearrange("(n p) c -> n p c", p=128)
    q_t = q_sp.ap().rearrange("(n p) c -> n p c", p=128)
    yp_t = yp_sp.ap().rearrange("(n p) c -> n p c", p=128)
    w1t_t = w1t_h.ap().rearrange("(n p) c -> n p c", p=128)
    w2t_t = w2t_h.ap().rearrange("(n p) c -> n p c", p=128)
    w1q_t = w1q_sp.ap().rearrange("(n p) c -> n p c", p=128)
    w2q_t = w2q_sp.ap().rearrange("(n p) c -> n p c", p=128)
    W_t = {nm: W_h[nm].ap().rearrange("(n p) c -> n p c", p=128) for nm in W_h}

    es = ExitStack()
    sems = {}
    semnames = ["AV", "AS", "AT", "BV", "BS", "BT", "CCS", "CV", "CS", "CT"]
    for g in GROUP_SEMS.values():
        semnames.extend(g)
    for name in semnames:
        sems[name] = es.enter_context(nc.semaphore(name))

    sb = lambda n, s, d: es.enter_context(nc.sbuf_tensor(n, s, d))
    xbuf = [sb(f"xbuf{j}", [128, C], F32) for j in range(4)]
    idb_s = sb("idb_s", [128, 128], BF16)
    rg_s = sb("rg_s", [128, NT], F32)
    mrgn_s = sb("mrgn_s", [128, NT], F32)
    invsw = [sb(f"invsw{m}", [128, 1], F32) for m in range(2)]
    swq = [sb(f"swq{m}", [128, 1], F32) for m in range(2)]
    s_pp = sb("s_pp", [128, 1], F32)
    b_pp = sb("b_pp", [128, 1], F32)
    row128 = sb("row128", [1, 128], F32)

    sched = {}
    counts = {}

    es_a = ExitStack()
    sba = lambda n, s, d: es_a.enter_context(nc.sbuf_tensor(n, s, d))
    W_sb = {nm: sba(nm + "_sb", [128, NF * C], BF16)
            for nm in ("Wq", "Wk0", "Wk1", "Wv0", "Wv1")}
    brow = {nm: sba(nm + "_row", [1, C], BF16)
            for nm in ("bq", "bk0", "bk1", "bv0", "bv1")}
    ones_r = sba("ones_r", [1, 128], BF16)
    xsq = sba("xsq", [128, C], F32)
    idf_s = sba("idf_s", [128, 128], F32)
    sacc = sba("sacc", [128, 2048], F32)
    sm_s = sba("sm_s", [128, NT], F32)
    ssq_s = sba("ssq_s", [128, NT], F32)
    xnbuf = [sba(f"xn{j}", [128, C], F32) for j in range(2)]
    xnfm = [sba(f"xnfm{j}", [128, C], BF16) for j in range(2)]
    qtm = [sba(f"qtm{j}", [128, C], BF16) for j in range(2)]
    ktm = [[sba(f"k{kb}tm{j}", [128, C], BF16) for j in range(2)]
           for kb in range(2)]
    vtm = [[sba(f"v{kb}tm{j}", [128, C], BF16) for j in range(2)]
           for kb in range(2)]
    wstg = [sba(f"wstg{j}", [128, 2048], F32) for j in range(2)]
    wqo = [sba(f"wqo{j}", [128, 2048], BF16) for j in range(2)]
    wzf = sba("wzf", [128, 2048], F32)
    wsum = sba("wsum", [128, 1], F32)
    duo = [sba(f"duo{j}", [128, 2], F32) for j in range(2)]
    sduo = [sba(f"sduo{j}", [128, 2], F32) for j in range(2)]
    tA = [sba(f"tA{j}", [128, 1], F32) for j in range(8)]

    ps_states = es_a.enter_context(nc.psum_tensor("ps_states", [128, 2048], F32))
    ps_rot = [es_a.enter_context(nc.psum_tensor(f"ps_rot{j}", [128, C], F32))
              for j in range(2)]

    pieces = []
    for f in range(NF):
        for h in range(2):
            pieces.append((0, w1t_t[f][:, 2048 * h:2048 * (h + 1)],
                           w1q_t[f][:, 2048 * h:2048 * (h + 1)], 2048))
    for h in range(NH2):
        pieces.append((1, w2t_t[h], w2q_t[h], 1024))
    NP1 = 16
    NUMEL = [C * HID, HID * C]
    p1_at = {t: [] for t in range(NT)}
    p2_at = {t: [] for t in range(NT)}
    red_at = {t: [] for t in range(NT)}
    if NT >= 32:
        for k in range(16):
            p1_at[k // 2].append(k)
        red_at[8].append(0)
        for k in range(16):
            p2_at[9 + k // 2].append(k)
        for k in range(32):
            p1_at[9 + k // 4].append(16 + k)
        red_at[17].append(1)
        for k in range(32):
            p2_at[18 + k // 3].append(16 + k)
    else:
        for k in range(16):
            p1_at[0].append(k)
        red_at[1].append(0)
        for k in range(16):
            p2_at[1].append(k)
        for k in range(32):
            p1_at[1].append(16 + k)
        red_at[2].append(1)
        for k in range(32):
            p2_at[2].append(16 + k)
    stg_ord = []
    for t in range(NT):
        for k in p1_at[t]:
            stg_ord.append(("1", k))
        for k in p2_at[t]:
            stg_ord.append(("2", k))
    stg_idx = {key: n for n, key in enumerate(stg_ord)}
    stg_consume = {("1", k): f"wc1_{k}" for _, k in stg_ord if _ == "1"}
    stg_consume.update({("2", k): f"wc2_{k}" for _, k in stg_ord if _ == "2"})

    def stg_buf(kind, k):
        return stg_idx[(kind, k)] % 2

    def stg_prev(kind, k):
        n = stg_idx[(kind, k)]
        return stg_consume[stg_ord[n - 2]] if n >= 2 else None

    p2_ord = [k for kind, k in stg_ord if kind == "2"]
    p2_pos = {k: i for i, k in enumerate(p2_ord)}

    def wqo_buf(k):
        return p2_pos[k] % 2

    def wqo_prev(k):
        i = p2_pos[k]
        return f"ws2_{p2_ord[i - 2]}" if i >= 2 else None

    def body1_sync(E):
        for t in range(NT):
            if t >= 2:
                E.wait(f"A_xn{t-2}", f"A_sq{t-2}")
            E.dma("ax", f"A_x{t}", out=xbuf[t % 4][:], in_=x_t[t])

    def body1_gpsimd(E):
        g = E.T
        E.dma("aw", "ld_idf", out=idf_s[:], in_=idf_d.ap())
        E.dma("aw", "ld_idb", out=idb_s[:], in_=idb_d.ap())
        E.dma("aw", "ld_ones", out=ones_r[:], in_=ones_d.ap())
        for nm in ("bq", "bk0", "bk1", "bv0", "bv1"):
            E.dma("aw", f"ld_{nm}", out=brow[nm][:], in_=vrow[nm].ap())
        E.dma("aw", "ld_acts", out=s_pp[:],
              in_=acts_h.ap().partition_broadcast(128))
        E.dma("aw", "ld_actb", out=b_pp[:],
              in_=actb_h.ap().partition_broadcast(128))
        for nm in ("Wq", "Wk0", "Wk1", "Wv0", "Wv1"):
            for c in range(NF):
                E.dma("aw", f"ldW_{nm}_{c}",
                      out=W_sb[nm][:, C * c:C * (c + 1)], in_=W_t[nm][c])
        for t in range(NT):
            for m in red_at[t]:
                E.wait(f"wsumd_{m}")
                E.dma("aw", f"redst_{m}",
                      out=red_d.ap().rearrange("o p -> p o"), in_=wsum[:])
                E.wait(f"redst_{m}")
                E.dma("aw", f"redld_{m}", out=row128[:], in_=red_d.ap())
                E.wait(f"cellr_{m}")
                E.dma("aw", f"cellst_{m}", out=cell_d.ap(), in_=tA[1][0:1, :])
                E.wait(f"cellst_{m}")
                E.dma("aw", f"swqld_{m}", out=swq[m][:],
                      in_=cell_d.ap().partition_broadcast(128))
            for k in p1_at[t]:
                m, src, dst, w = pieces[k]
                E.wait(stg_prev("1", k))
                E.dma("aw", f"wl1_{k}",
                      out=wstg[stg_buf("1", k)][:, :w], in_=src)
            for k in p2_at[t]:
                m, src, dst, w = pieces[k]
                E.wait(stg_prev("2", k))
                E.dma("aw", f"wl2_{k}",
                      out=wstg[stg_buf("2", k)][:, :w], in_=src)
                E.wait(f"wc2_{k}")
                E.dma("aw", f"ws2_{k}", out=dst, in_=wqo[wqo_buf(k)][:, :w])
            E.wait(f"A_evq{t}")
            E.dma("ag", f"A_sp{t}", out=q_t[t], in_=qtm[t % 2][:])
        E.wait("sacc_evac")
        E.dma("cc", "ar_in", out=cc_in.ap(), in_=sacc[:])

    def body1_gpsimd_end(E):
        pass

    def body1_vector(E):
        v = E.T
        E.inc(v.memset(wsum[:], 0.0), "AV", 1)
        for nm in ("Wq", "Wk0", "Wk1", "Wv0", "Wv1"):
            E.wait("ld_idb")
            for c in range(NF):
                E.wait(f"ldW_{nm}_{c}")
                d0 = C * c + 128 * c
                E.inc(v.tensor_tensor(W_sb[nm][:, d0:d0 + 128],
                                      W_sb[nm][:, d0:d0 + 128], idb_s[:],
                                      OP.add), "AV", 1)
        E.inc(v.engine_nop(), "AV", 1, "setup_v_done")

        def stats1(s):
            j = s % 2
            E.wait(f"A_x{s}")
            E.inc(v.reduce_sum(sm_s[:, s:s + 1], xbuf[s % 4][:], AX), "AV", 1)
            E.wait(f"A_sq{s}", None if s < 2 else f"A_sqd{s-2}")
            m, t1, mm, var, rve, msq = tA[0], tA[1], tA[2], tA[3], tA[4], tA[5]
            E.inc(v.tensor_scalar(m[:], sm_s[:, s:s + 1], 1.0 / C, None,
                                  OP.mult), "AV", 1)
            E.inc(v.tensor_scalar(t1[:], ssq_s[:, s:s + 1], 1.0 / C, None,
                                  OP.mult), "AV", 1)
            E.inc(v.tensor_tensor(mm[:], m[:], m[:], OP.mult), "AV", 1)
            E.inc(v.tensor_tensor(var[:], t1[:], mm[:], OP.subtract), "AV", 1)
            E.inc(v.tensor_scalar(duo[j][:, 0:1], var[:], 1e-5, None, OP.add),
                  "AV", 1)
            E.inc(v.reciprocal(rve[:], duo[j][:, 0:1]), "AV", 1)
            E.inc(v.tensor_tensor(msq[:], var[:], rve[:], OP.mult), "AV", 1)
            E.inc(v.tensor_scalar(duo[j][:, 1:2], msq[:], 1e-8, None, OP.add),
                  "AV", 1, f"A_duo{s}")
            E.inc(v.tensor_copy(tA[6][:], m[:]), "AV", 1)

        def stats2(s):
            j = s % 2
            E.wait(f"A_sqd{s}", None if s < 2 else f"A_tp{s-2}")
            rr, gg, mr, mrn = tA[0], tA[7], tA[2], tA[3]
            E.inc(v.reciprocal(rr[:], sduo[j][:, 0:1]), "AV", 1)
            E.inc(v.reciprocal(gg[:], sduo[j][:, 1:2]), "AV", 1)
            E.inc(v.tensor_tensor(rg_s[:, s:s + 1], rr[:], gg[:], OP.mult),
                  "AV", 1)
            E.inc(v.tensor_tensor(mr[:], tA[6][:], rr[:], OP.mult), "AV", 1)
            E.inc(v.tensor_scalar(mrn[:], mr[:], -1.0, None, OP.mult), "AV", 1)
            E.inc(v.tensor_scalar(mrgn_s[:, s:s + 1], mr[:], gg[:], -1.0,
                                  OP.mult, OP.mult), "AV", 1)
            E.inc(v.tensor_scalar(xnbuf[j][:], xbuf[s % 4][:], rr[:], mrn[:],
                                  OP.mult, OP.add), "AV", 1, f"A_xn{s}")

        stats1(0)
        stats2(0)
        for t in range(NT):
            if t + 1 < NT:
                stats1(t + 1)
            for m_ in red_at[t]:
                E.wait(f"redld_{m_}")
                E.inc(v.reduce_sum(tA[1][0:1, :], row128[:], AX), "AV", 1)
                E.inc(v.tensor_scalar(tA[1][0:1, :], tA[1][0:1, :],
                                      1.0 / NUMEL[m_], None, OP.mult), "AV", 1)
                E.inc(v.tensor_scalar(tA[1][0:1, :], tA[1][0:1, :], 1e-5,
                                      None, OP.max), "AV", 1)
                E.inc(v.reciprocal(tA[1][0:1, :], tA[1][0:1, :]), "AV", 1,
                      f"cellr_{m_}")
                E.wait(f"swqld_{m_}")
                E.inc(v.reciprocal(invsw[m_][:], swq[m_][:]), "AV", 1)
                E.inc(v.memset(wsum[:], 0.0), "AV", 1, f"wred_{m_}")
            for k in p1_at[t]:
                m_, src, dst, w = pieces[k]
                E.wait(f"wl1_{k}")
                E.inc(v.reduce_sum(tA[1][:], wstg[stg_buf("1", k)][:, :w], AX,
                                   apply_absolute_value=True), "AV", 1)
                E.inc(v.tensor_tensor(wsum[:], wsum[:], tA[1][:], OP.add),
                      "AV", 1, f"wc1_{k}")
                if k == NP1 - 1 or k == NP1 + 32 - 1:
                    E.inc(v.engine_nop(), "AV", 1,
                          f"wsumd_{0 if k < NP1 else 1}")
            for k in p2_at[t]:
                m_, src, dst, w = pieces[k]
                E.wait(f"wl2_{k}", f"wred_{m_}", wqo_prev(k))
                E.inc(v.tensor_scalar(wzf[:, :w], wstg[stg_buf("2", k)][:, :w],
                                      swq[m_][:], MAGIC, OP.mult, OP.add),
                      "AV", 1)
                E.inc(v.tensor_scalar(wzf[:, :w], wzf[:, :w], -MAGIC, None,
                                      OP.add), "AV", 1)
                E.inc(v.tensor_scalar(wqo[wqo_buf(k)][:, :w], wzf[:, :w], 1.0,
                                      -1.0, OP.min, OP.max), "AV", 1,
                      f"wc2_{k}")
            if t + 1 < NT:
                stats2(t + 1)
        E.wait(f"A_s1{NT-1}", f"A_s0{NT-1}")
        E.inc(v.tensor_copy(sacc[:, 0:1024], ps_states[:, 0:1024]), "AV", 1)
        E.inc(v.tensor_copy(sacc[:, 1024:2048], ps_states[:, 1024:2048]),
              "AV", 1, "sacc_evac")

    def body1_scalar(E):
        s = E.T
        E.wait(f"A_x0")
        E.inc(s.activation(xsq[:], xbuf[0][:], AF.Square,
                           accum_out=ssq_s[:, 0:1]), "AS", 1, "A_sq0")
        E.wait("A_duo0")
        E.inc(s.activation(sduo[0][:], duo[0][:], AF.Sqrt), "AS", 1, "A_sqd0")
        for t in range(NT + 1):
            if t + 1 < NT:
                tn = t + 1
                E.wait(f"A_x{tn}")
                E.inc(s.activation(xsq[:], xbuf[tn % 4][:], AF.Square,
                                   accum_out=ssq_s[:, tn:tn + 1]), "AS", 1,
                      f"A_sq{tn}")
            if t >= 1:
                tp = t - 1
                jp = tp % 2
                E.wait(f"A_q{tp}", None if tp < 2 else f"A_sp{tp-2}")
                E.inc(s.activation(qtm[jp][:], ps_rot[1][:], AF.Identity),
                      "AS", 1, f"A_evq{tp}")
                E.wait(f"A_k0{tp}", None if tp < 2 else f"A_s0{tp-2}")
                E.inc(s.activation(ktm[0][jp][:], ps_rot[0][:], AF.Identity),
                      "AS", 1, f"A_evk0{tp}")
                E.wait(f"A_v0{tp}", None if tp < 2 else f"A_s0{tp-2}")
                E.inc(s.activation(vtm[0][jp][:], ps_rot[1][:], AF.Identity),
                      "AS", 1, f"A_evv0{tp}")
                E.wait(f"A_k1{tp}", None if tp < 2 else f"A_s1{tp-2}")
                E.inc(s.activation(ktm[1][jp][:], ps_rot[0][:], AF.Identity),
                      "AS", 1, f"A_evk1{tp}")
                E.wait(f"A_v1{tp}", None if tp < 2 else f"A_s1{tp-2}")
                E.inc(s.activation(vtm[1][jp][:], ps_rot[1][:], AF.Identity),
                      "AS", 1, f"A_evv1{tp}")
            if t < NT:
                E.wait(f"A_tp{t}", None if t < 2 else f"A_v1{t-2}")
                E.inc(s.activation(xnfm[t % 2][:], ps_rot[0][:], AF.Identity),
                      "AS", 1, f"A_evtp{t}")
                if t + 1 < NT:
                    tn = t + 1
                    E.wait(f"A_duo{tn}", f"A_xn{tn-2}" if tn >= 2 else None)
                    E.inc(s.activation(sduo[tn % 2][:], duo[tn % 2][:],
                                       AF.Sqrt), "AS", 1, f"A_sqd{tn}")

    _BIAS = {"Wq": "bq", "Wk0": "bk0", "Wk1": "bk1", "Wv0": "bv0",
             "Wv1": "bv1"}

    def _proj(E, T, dst_ps, src_fm, Wname, t, ev):
        for f in range(NF):
            T.ldweights(src_fm[:, 128 * f:128 * (f + 1)])
            for h in range(2):
                T.matmul(dst_ps[:, 512 * h:512 * (h + 1)],
                         src_fm[:, 128 * f:128 * (f + 1)],
                         W_sb[Wname][:, C * f + 512 * h:C * f + 512 * (h + 1)],
                         start=(f == 0), stop=False)
        b = brow[_BIAS[Wname]]
        T.ldweights(ones_r[:])
        for h in range(2):
            ins = T.matmul(dst_ps[:, 512 * h:512 * (h + 1)], ones_r[:],
                           b[:, 512 * h:512 * (h + 1)],
                           start=False, stop=True)
        E.inc(ins, "AT", 1, ev)

    def body1_tensor(E):
        T = E.T
        E.wait("A_xn0", "setup_v_done", "ld_idf")
        ins = None
        for c in range(NF):
            ins = T.transpose(ps_rot[0][:, 128 * c:128 * (c + 1)],
                              xnbuf[0][:, 128 * c:128 * (c + 1)], idf_s[:])
        E.inc(ins, "AT", 1, "A_tp0")
        for u in range(1, NT + 1):
            t = u - 1
            j = t % 2
            E.wait(f"A_evtp{t}", None if t < 1 else f"A_evv1{t-1}")
            _proj(E, T, ps_rot[1], xnfm[j], "Wq", t, f"A_q{t}")
            _proj(E, T, ps_rot[0], xnfm[j], "Wk0", t, f"A_k0{t}")
            E.wait(f"A_evq{t}")
            _proj(E, T, ps_rot[1], xnfm[j], "Wv0", t, f"A_v0{t}")
            E.wait(f"A_evv0{t}")
            for p in range(8):
                T.ldweights(ktm[0][j][:, 128 * p:128 * (p + 1)])
                ins = T.matmul(ps_states[:, 128 * p:128 * (p + 1)],
                               ktm[0][j][:, 128 * p:128 * (p + 1)],
                               vtm[0][j][:, 128 * p:128 * (p + 1)],
                               start=(t == 0 and p % 4 == 0),
                               stop=(t == NT - 1), skip_group_check=True)
            E.inc(ins, "AT", 1, f"A_s0{t}")
            E.wait(f"A_evk0{t}")
            _proj(E, T, ps_rot[0], xnfm[j], "Wk1", t, f"A_k1{t}")
            _proj(E, T, ps_rot[1], xnfm[j], "Wv1", t, f"A_v1{t}")
            if t + 1 < NT:
                E.wait(f"A_xn{t+1}", f"A_evk1{t}")
                for c in range(NF):
                    ins = T.transpose(
                        ps_rot[0][:, 128 * c:128 * (c + 1)],
                        xnbuf[(t + 1) % 2][:, 128 * c:128 * (c + 1)], idf_s[:])
                E.inc(ins, "AT", 1, f"A_tp{t+1}")
            E.wait(f"A_evv1{t}")
            for p in range(8):
                T.ldweights(ktm[1][j][:, 128 * p:128 * (p + 1)])
                ins = T.matmul(ps_states[:, 1024 + 128 * p:1024 + 128 * (p + 1)],
                               ktm[1][j][:, 128 * p:128 * (p + 1)],
                               vtm[1][j][:, 128 * p:128 * (p + 1)],
                               start=(t == 0 and p % 4 == 0),
                               stop=(t == NT - 1), skip_group_check=True)
            E.inc(ins, "AT", 1, f"A_s1{t}")

    body1 = [("sync", body1_sync), ("gpsimd", body1_gpsimd),
             ("vector", body1_vector), ("scalar", body1_scalar),
             ("tensor", body1_tensor)]

    A_end = [f"A_s1{NT-1}", "sacc_evac", f"A_sqd{NT-1}",
             f"A_sp{NT-1}", f"ws2_{NP1 + 32 - 1}", f"A_x{NT-1}"]

    es_b = ExitStack()
    Bt = {}

    def alloc_b():
        sbb = lambda n, s, d: es_b.enter_context(nc.sbuf_tensor(n, s, d))
        Bt["w1q"] = sbb("w1q_sb", [128, NF * HID], BF16)
        Bt["w2q"] = sbb("w2q_sb", [128, NH2 * C], BF16)
        Bt["n1w"] = sbb("n1w_rep", [128, C], F32)
        Bt["n2w"] = sbb("n2w_rep", [128, HID], BF16)
        Bt["hact"] = sbb("hact", [128, HID], BF16)
        Bt["u"] = sbb("u_buf", [128, 1024], F32)
        Bt["xh"] = sbb("xh", [128, C], F32)
        Bt["xq"] = sbb("xq", [128, C], BF16)
        Bt["xqfm"] = sbb("xqfm", [128, C], BF16)
        Bt["xq2tm"] = [sbb(f"xq2tm{j}", [128, HID], BF16) for j in range(2)]
        Bt["xq2fm"] = sbb("xq2fm", [128, HID], BF16)
        Bt["qs1"] = [sbb(f"qs1_{j}", [128, 1], F32) for j in range(2)]
        Bt["beta"] = [sbb(f"beta{j}", [128, 1], F32) for j in range(2)]
        Bt["alpha"] = [sbb(f"alpha{j}", [128, 1], F32) for j in range(3)]
        Bt["qs2"] = sbb("qs2", [128, 1], F32)
        Bt["sqacc"] = sbb("sqacc", [128, 4], F32)
        Bt["geB"] = [sbb(f"geB{j}", [128, 1], F32) for j in range(2)]
        Bt["sgeB"] = [sbb(f"sgeB{j}", [128, 1], F32) for j in range(2)]
        Bt["tB"] = [sbb(f"tB{j}", [128, 1], F32) for j in range(4)]
        Bt["ps_h"] = [es_b.enter_context(
            nc.psum_tensor(f"ps_h{j}", [128, 1024], F32)) for j in range(2)]
        Bt["ps_tpB"] = [es_b.enter_context(
            nc.psum_tensor(f"ps_tpB{j}", [128, 1024], BF16)) for j in range(2)]
        Bt["ps_y2"] = es_b.enter_context(
            nc.psum_tensor("ps_y2", [128, 1024], F32))

    def body2_sync(E):
        s = E.T
        E.wait(*A_end)
        for f in range(NF):
            E.dma("bw", f"B_w1l{f}",
                  out=Bt["w1q"][:, HID * f:HID * (f + 1)], in_=w1q_t[f])
        for h in range(NH2):
            E.dma("bw", f"B_w2l{h}",
                  out=Bt["w2q"][:, C * h:C * (h + 1)], in_=w2q_t[h])
        for t in range(NT):
            if t >= 4:
                E.wait(f"B_yp{t-4}")
            E.dma("bx", f"B_x{t}", out=xbuf[t % 4][:], in_=x_t[t])

    def body2_gpsimd(E):
        g = E.T
        E.wait(*A_end)
        E.dma("cc", "B_n1w", out=Bt["n1w"][:],
              in_=vrow["n1w"].ap().partition_broadcast(128))
        E.dma("cc", "B_n2w", out=Bt["n2w"][:],
              in_=n2w_h.ap().partition_broadcast(128))
        E.wait("ar_in")
        E.inc(g.collective_compute(
            "AllReduce", OP.add,
            replica_groups=[[0, 1], [2, 3], [4, 5], [6, 7]],
            ins=[cc_in.ap()], outs=[cc_out.ap()]), "CCS", 1, "ar_cc")

    def _b_xq_step(E, v, tn):
        jn = tn % 2
        E.wait(f"B_x{tn}", "B_n1w",
               None if tn < 2 else f"B_relu{tn-2}_3",
               None if tn < 4 else f"B_ydma{tn-4}",
               None if tn < 1 else f"B_tpq{tn-1}")
        E.inc(v.tensor_scalar(Bt["xh"][:], xbuf[tn % 4][:], rg_s[:, tn:tn + 1],
                              mrgn_s[:, tn:tn + 1], OP.mult, OP.add), "BV", 1)
        E.inc(v.tensor_tensor(Bt["xh"][:], Bt["xh"][:], Bt["n1w"][:],
                              OP.mult), "BV", 1)
        E.inc(v.reduce_max(Bt["tB"][0][:], Bt["xh"][:], AX,
                           apply_absolute_value=True), "BV", 1)
        E.inc(v.tensor_scalar(Bt["tB"][0][:], Bt["tB"][0][:], 1e-5, None,
                              OP.max), "BV", 1)
        E.inc(v.reciprocal(Bt["tB"][1][:], Bt["tB"][0][:]), "BV", 1)
        E.inc(v.tensor_scalar(Bt["qs1"][jn][:], Bt["tB"][1][:], 127.0, None,
                              OP.mult), "BV", 1)
        E.inc(v.reciprocal(Bt["tB"][2][:], Bt["qs1"][jn][:]), "BV", 1)
        E.inc(v.tensor_tensor(Bt["beta"][jn][:], Bt["tB"][2][:], invsw[0][:],
                              OP.mult), "BV", 1)
        E.inc(v.tensor_scalar(Bt["xh"][:], Bt["xh"][:], Bt["qs1"][jn][:],
                              MAGIC, OP.mult, OP.add), "BV", 1)
        E.inc(v.tensor_scalar(Bt["xq"][:], Bt["xh"][:], -MAGIC, None,
                              OP.add), "BV", 1, f"B_xq{tn}")

    def body2_vector(E):
        v = E.T
        E.wait(*A_end)
        _b_xq_step(E, v, 0)
        for t in range(NT + 2):
            j = t % 2
            if t < NT:
                E.wait(f"B_tpq{t}")
                E.inc(v.tensor_copy(Bt["xqfm"][:], Bt["ps_tpB"][0][:]), "BV",
                      1, f"B_xqfm{t}")
            if t >= 2 and t - 2 < NT:
                tr = t - 2
                for q in range(4):
                    E.wait(f"B_tpxq2_{tr}_{q}")
                    E.inc(v.tensor_copy(
                        Bt["xq2fm"][:, 1024 * q:1024 * (q + 1)],
                        Bt["ps_tpB"][(q + 1) % 2][:]), "BV", 1,
                        f"B_xq2fm{tr}_{q}")
            if t + 1 < NT:
                _b_xq_step(E, v, t + 1)
            if t < NT:
                E.wait(f"B_sqacc{t}_3")
                E.inc(v.tensor_tensor(Bt["tB"][0][:], Bt["sqacc"][:, 0:1],
                                      Bt["sqacc"][:, 1:2], OP.add), "BV", 1)
                E.inc(v.tensor_tensor(Bt["tB"][1][:], Bt["sqacc"][:, 2:3],
                                      Bt["sqacc"][:, 3:4], OP.add), "BV", 1)
                E.inc(v.tensor_tensor(Bt["tB"][0][:], Bt["tB"][0][:],
                                      Bt["tB"][1][:], OP.add), "BV", 1)
                E.inc(v.tensor_scalar(Bt["geB"][j][:], Bt["tB"][0][:],
                                      1.0 / HID, 1e-8, OP.mult, OP.add),
                      "BV", 1, f"B_ge{t}")
                E.wait(f"B_sge{t}", "B_n2w")
                E.inc(v.reciprocal(Bt["tB"][2][:], Bt["sgeB"][j][:]), "BV", 1)
                for q in range(4):
                    E.inc(v.scalar_tensor_tensor(
                        Bt["hact"][:, 1024 * q:1024 * (q + 1)],
                        Bt["hact"][:, 1024 * q:1024 * (q + 1)], Bt["tB"][2][:],
                        Bt["n2w"][:, 1024 * q:1024 * (q + 1)], OP.mult,
                        OP.mult), "BV", 1)
                E.inc(v.reduce_max(Bt["tB"][0][:], Bt["hact"][:], AX,
                                   apply_absolute_value=True), "BV", 1)
                E.inc(v.tensor_scalar(Bt["tB"][0][:], Bt["tB"][0][:], 1e-5,
                                      None, OP.max), "BV", 1)
                E.inc(v.reciprocal(Bt["tB"][1][:], Bt["tB"][0][:]), "BV", 1)
                E.inc(v.tensor_scalar(Bt["qs2"][:], Bt["tB"][1][:], 127.0,
                                      None, OP.mult), "BV", 1)
                E.inc(v.reciprocal(Bt["tB"][3][:], Bt["qs2"][:]), "BV", 1)
                E.inc(v.tensor_tensor(Bt["alpha"][t % 3][:], Bt["tB"][3][:],
                                      invsw[1][:], OP.mult), "BV", 1)
                for q in range(4):
                    E.inc(v.tensor_scalar(Bt["u"][:],
                                          Bt["hact"][:, 1024 * q:1024 * (q + 1)],
                                          Bt["qs2"][:], MAGIC, OP.mult,
                                          OP.add), "BV", 1)
                    E.inc(v.tensor_scalar(
                        Bt["xq2tm"][t % 2][:, 1024 * q:1024 * (q + 1)],
                        Bt["u"][:], -MAGIC, None, OP.add), "BV", 1,
                        f"B_xq2_{t}" if q == 3 else None)
            if t >= 2 and t - 2 < NT:
                tr = t - 2
                E.wait(f"B_w2_{tr}", None if tr < 1 else f"B_ydma{tr-1}")
                E.inc(v.scalar_tensor_tensor(
                    Bt["xh"][:], Bt["ps_y2"][:], Bt["alpha"][tr % 3][:],
                    xbuf[tr % 4][:], OP.mult, OP.add), "BV", 1, f"B_yp{tr}")

    def body2_scalar(E):
        s = E.T
        E.wait(*A_end)
        for t in range(NT + 2):
            j = t % 2
            if t < NT:
                for q in range(4):
                    E.wait(f"B_w1_{t}_{q}",
                           f"B_xq2_{t-1}" if (q == 0 and t >= 1) else None)
                    E.inc(s.activation(Bt["u"][:], Bt["ps_h"][q % 2][:],
                                       AF.Relu, scale=Bt["beta"][j][:]),
                          "BS", 1, f"B_relu{t}_{q}")
                    E.wait(None if t < 1 else f"B_xq2_{t-1}")
                    E.inc(s.activation(Bt["hact"][:, 1024 * q:1024 * (q + 1)],
                                       Bt["u"][:], AF.Square), "BS", 1)
                    E.inc(s.activation(Bt["hact"][:, 1024 * q:1024 * (q + 1)],
                                       Bt["hact"][:, 1024 * q:1024 * (q + 1)],
                                       AF.Identity, bias=b_pp[:],
                                       scale=s_pp[:]), "BS", 1)
                    E.inc(s.activation(Bt["u"][:],
                                       Bt["hact"][:, 1024 * q:1024 * (q + 1)],
                                       AF.Square,
                                       accum_out=Bt["sqacc"][:, q:q + 1]),
                          "BS", 1, f"B_sqacc{t}_{q}")
                E.wait(f"B_ge{t}")
                E.inc(s.activation(Bt["sgeB"][j][:], Bt["geB"][j][:], AF.Sqrt),
                      "BS", 1, f"B_sge{t}")
            if t >= 2 and t - 2 < NT:
                tr = t - 2
                E.wait(f"B_yp{tr}")
                E.dma("by", f"B_ydma{tr}", out=yp_t[tr], in_=Bt["xh"][:])

    def _b_w2(E, T, t):
        E.wait(f"B_xq2fm{t}_3", None if t < 1 else f"B_yp{t-1}",
               f"B_w2l{NH2-1}" if t == 0 else None)
        for h in range(NH2):
            T.ldweights(Bt["xq2fm"][:, 128 * h:128 * (h + 1)])
            for g2 in range(2):
                ins = T.matmul(
                    Bt["ps_y2"][:, 512 * g2:512 * (g2 + 1)],
                    Bt["xq2fm"][:, 128 * h:128 * (h + 1)],
                    Bt["w2q"][:, C * h + 512 * g2:C * h + 512 * (g2 + 1)],
                    start=(h == 0), stop=(h == NH2 - 1))
        E.inc(ins, "BT", 1, f"B_w2_{t}")

    def body2_tensor(E):
        T = E.T
        E.wait(*A_end, f"B_w1l{NF-1}")
        for t in range(NT + 2):
            if t < NT:
                E.wait(f"B_xq{t}",
                       None if t < 3 else f"B_xq2fm{t-3}_3")
                for c in range(NF):
                    ins = T.transpose(Bt["ps_tpB"][0][:, 128 * c:128 * (c + 1)],
                                      Bt["xq"][:, 128 * c:128 * (c + 1)],
                                      idb_s[:])
                E.inc(ins, "BT", 1, f"B_tpq{t}")
                for q in range(4):
                    E.wait(f"B_xqfm{t}" if q == 0 else None,
                           (f"B_relu{t}_{q-2}" if q >= 2 else
                            (f"B_relu{t-1}_{q+2}" if t >= 1 else None)))
                    for f in range(NF):
                        T.ldweights(Bt["xqfm"][:, 128 * f:128 * (f + 1)])
                        for g2 in range(2):
                            ins = T.matmul(
                                Bt["ps_h"][q % 2][:, 512 * g2:512 * (g2 + 1)],
                                Bt["xqfm"][:, 128 * f:128 * (f + 1)],
                                Bt["w1q"][:, HID * f + 1024 * q + 512 * g2:
                                           HID * f + 1024 * q + 512 * (g2 + 1)],
                                start=(f == 0), stop=(f == NF - 1))
                    E.inc(ins, "BT", 1, f"B_w1_{t}_{q}")
            if t >= 2 and t - 2 < NT:
                tr = t - 2
                for q in range(4):
                    bk = (q + 1) % 2
                    if q == 0:
                        E.wait(f"B_xq2_{tr}",
                               None if tr < 1 else f"B_xq2fm{tr-1}_2")
                    elif q == 1:
                        E.wait(f"B_xqfm{t}" if t < NT
                               else f"B_xq2fm{tr-1}_3")
                    else:
                        E.wait(f"B_xq2fm{tr}_{q-2}")
                    for c in range(8):
                        ins = T.transpose(
                            Bt["ps_tpB"][bk][:, 128 * c:128 * (c + 1)],
                            Bt["xq2tm"][tr % 2][:, 1024 * q + 128 * c:
                                                1024 * q + 128 * (c + 1)],
                            idb_s[:])
                    E.inc(ins, "BT", 1, f"B_tpxq2_{tr}_{q}")
                _b_w2(E, T, tr)

    body2 = [("sync", body2_sync), ("gpsimd", body2_gpsimd),
             ("vector", body2_vector), ("scalar", body2_scalar),
             ("tensor", body2_tensor)]

    B_end = [f"B_w2_{NT-1}", f"B_yp{NT-1}", f"B_ydma{NT-1}",
             f"B_sge{NT-1}", f"B_x{NT-1}"]

    es_c = ExitStack()
    Ct = {}

    def alloc_c():
        sbc = lambda n, s, d: es_c.enter_context(nc.sbuf_tensor(n, s, d))
        Ct["Wo"] = sbc("Wo_sb", [128, NF * C], BF16)
        Ct["lng"] = sbc("lng_rep", [128, C], F32)
        Ct["lnb"] = sbc("lnb_rep", [128, C], F32)
        Ct["bo"] = sbc("bo_rep", [128, C], F32)
        Ct["qb"] = [sbc(f"qbC{j}", [128, C], BF16) for j in range(2)]
        Ct["ypb"] = [sbc(f"ypbC{j}", [128, C], F32) for j in range(3)]
        Ct["qfm"] = sbc("qfmC", [128, C], BF16)
        Ct["ot"] = [sbc(f"otC{j}", [128, C], F32) for j in range(3)]
        Ct["m2"] = [sbc(f"m2C{j}", [128, 1], F32) for j in range(2)]
        Ct["osc"] = sbc("oscC", [128, C], F32)
        Ct["oln"] = [sbc(f"olnC{j}", [128, C], BF16) for j in range(2)]
        Ct["olnfm"] = sbc("olnfmC", [128, C], BF16)
        Ct["yb"] = [sbc(f"ybC{j}", [128, C], F32) for j in range(2)]
        Ct["so"] = sbc("soC", [128, 1], F32)
        Ct["so2"] = [sbc(f"so2C{j}", [128, 1], F32) for j in range(2)]
        Ct["ve"] = [sbc(f"veC{j}", [128, 1], F32) for j in range(2)]
        Ct["sve"] = [sbc(f"sveC{j}", [128, 1], F32) for j in range(2)]
        Ct["tC"] = [sbc(f"tC{j}", [128, 1], F32) for j in range(4)]
        Ct["sres"] = sbc("sresC", [128, 2048], F32)
        Ct["state"] = sbc("state_sb", [128, C], BF16)
        Ct["ps_tp"] = [es_c.enter_context(
            nc.psum_tensor(f"ps_tpC{j}", [128, C], BF16)) for j in range(2)]
        Ct["ps_out"] = es_c.enter_context(
            nc.psum_tensor("ps_outC", [128, C], F32))
        Ct["ps_yw"] = es_c.enter_context(
            nc.psum_tensor("ps_ywC", [128, C], F32))

    def body3_sync(E):
        s = E.T
        E.wait(*B_end)
        for t in range(NT):
            if t >= 2:
                E.wait(f"C_qfm{t-2}")
            E.dma("cx", f"C_ql{t}", out=Ct["qb"][t % 2][:], in_=q_t[t])
        E.wait(f"C_ydma{NT-1}", f"C_ydma{NT-2}")

    def body3_gpsimd(E):
        g = E.T
        E.wait(*B_end, "ar_cc")
        E.dma("cc", "ar_done", out=Ct["sres"][:], in_=cc_out.ap())
        for c in range(NF):
            E.dma("aw", f"C_Wo{c}", out=Ct["Wo"][:, C * c:C * (c + 1)],
                  in_=W_t["Wo"][c])
        for nm in ("lng", "lnb", "bo"):
            E.dma("aw", f"C_{nm}", out=Ct[nm][:],
                  in_=vrow[nm].ap().partition_broadcast(128))
        for t in range(NT):
            if t >= 3:
                E.wait(f"C_y{t-3}")
            E.dma("cz", f"C_ypl{t}", out=Ct["ypb"][t % 3][:], in_=yp_t[t])
            E.wait(f"C_y{t}")
            E.dma("cy", f"C_ydma{t}", out=y_t[t], in_=Ct["yb"][t % 2][:])

    def body3_vector(E):
        v = E.T
        E.wait(*B_end, "C_lng", "C_lnb", "C_bo", "ar_done")
        E.inc(v.tensor_tensor(Ct["sres"][:, 0:1024], Ct["sres"][:, 0:1024],
                              Ct["sres"][:, 1024:2048], OP.mult), "CV", 1)
        E.inc(v.tensor_scalar(Ct["state"][:], Ct["sres"][:, 0:1024],
                              1.0 / 16777216.0, None, OP.mult), "CV", 1)
        ins = None
        for p in range(8):
            v.memset(Ct["state"][0:64, 128 * p + 64:128 * (p + 1)], 0.0)
            ins = v.memset(Ct["state"][64:128, 128 * p:128 * p + 64], 0.0)
        E.inc(ins, "CV", 1, "state_ready")
        for t in range(NT + 3):
            if t >= 1 and t - 1 < NT:
                s_ = t - 1
                j = s_ % 2
                E.wait(f"C_ot{s_}")
                E.inc(v.reduce_sum(Ct["so"][:], Ct["ot"][s_ % 3][:], AX),
                      "CV", 1)
                E.wait(f"C_sq{s_}")
                m2, vv_, mm2 = Ct["m2"][j], Ct["tC"][1], Ct["tC"][2]
                E.inc(v.tensor_scalar(m2[:], Ct["so"][:], 1.0 / C, None,
                                      OP.mult), "CV", 1)
                E.inc(v.tensor_scalar(vv_[:], Ct["so2"][j][:], 1.0 / C, None,
                                      OP.mult), "CV", 1)
                E.inc(v.tensor_tensor(mm2[:], m2[:], m2[:], OP.mult), "CV", 1)
                E.inc(v.tensor_tensor(vv_[:], vv_[:], mm2[:], OP.subtract),
                      "CV", 1)
                E.inc(v.tensor_scalar(Ct["ve"][j][:], vv_[:], 1e-5, None,
                                      OP.add), "CV", 1, f"C_ve{s_}")
            if t >= 2 and t - 2 < NT:
                s2 = t - 2
                j2 = s2 % 2
                ob = Ct["ot"][s2 % 3]
                E.wait(f"C_sve{s2}", None if s2 < 2 else f"C_tpln{s2-2}")
                rstd, mn = Ct["tC"][1], Ct["tC"][2]
                E.inc(v.reciprocal(rstd[:], Ct["sve"][j2][:]), "CV", 1)
                E.inc(v.tensor_scalar(mn[:], Ct["m2"][j2][:], rstd[:], -1.0,
                                      OP.mult, OP.mult), "CV", 1)
                E.inc(v.tensor_scalar(ob[:], ob[:], rstd[:],
                                      mn[:], OP.mult, OP.add), "CV", 1)
                E.inc(v.tensor_tensor(ob[:], ob[:],
                                      Ct["lng"][:], OP.mult), "CV", 1)
                E.inc(v.tensor_tensor(Ct["oln"][j2][:], ob[:],
                                      Ct["lnb"][:], OP.add), "CV", 1,
                      f"C_oln{s2}")
            if t >= 3:
                tr = t - 3
                E.wait(f"C_wo{tr}", f"C_ypl{tr}",
                       None if tr < 2 else f"C_ydma{tr-2}")
                jp = tr % 2
                E.inc(v.tensor_tensor(Ct["yb"][jp][:], Ct["ps_yw"][:],
                                      Ct["ypb"][tr % 3][:], OP.add), "CV", 1)
                E.inc(v.tensor_tensor(Ct["yb"][jp][:], Ct["yb"][jp][:],
                                      Ct["bo"][:], OP.add), "CV", 1,
                      f"C_y{tr}")

    def body3_scalar(E):
        s = E.T
        E.wait(*B_end)
        for t in range(NT + 3):
            if t < NT:
                E.wait(f"C_tpq{t}")
                E.inc(s.activation(Ct["qfm"][:], Ct["ps_tp"][0][:],
                                   AF.Identity), "CS", 1, f"C_qfm{t}")
            if t >= 3 and t - 3 < NT:
                tr = t - 3
                E.wait(f"C_tpln{tr}")
                E.inc(s.activation(Ct["olnfm"][:], Ct["ps_tp"][1][:],
                                   AF.Identity), "CS", 1, f"C_olnfm{tr}")
            if t < NT:
                j3 = t % 3
                E.wait(f"C_out{t}", None if t < 3 else f"C_oln{t-3}")
                E.inc(s.activation(Ct["ot"][j3][:], Ct["ps_out"][:],
                                   AF.Identity), "CS", 1, f"C_ot{t}")
                E.wait(None if t < 2 else f"C_ve{t-2}")
                E.inc(s.activation(Ct["osc"][:], Ct["ot"][j3][:], AF.Square,
                                   accum_out=Ct["so2"][t % 2][:]), "CS", 1,
                      f"C_sq{t}")
            if t >= 1 and t - 1 < NT:
                s_ = t - 1
                E.wait(f"C_ve{s_}")
                E.inc(s.activation(Ct["sve"][s_ % 2][:], Ct["ve"][s_ % 2][:],
                                   AF.Sqrt), "CS", 1, f"C_sve{s_}")

    def body3_tensor(E):
        T = E.T
        E.wait(*B_end, f"C_Wo{NF-1}")
        for t in range(NT + 3):
            if t < NT:
                E.wait(f"C_ql{t}", None if t < 1 else f"C_qfm{t-1}")
                for c in range(NF):
                    ins = T.transpose(Ct["ps_tp"][0][:, 128 * c:128 * (c + 1)],
                                      Ct["qb"][t % 2][:, 128 * c:128 * (c + 1)],
                                      idb_s[:])
                E.inc(ins, "CT", 1, f"C_tpq{t}")
            if t >= 3:
                tr = t - 3
                E.wait(f"C_oln{tr}", None if tr < 1 else f"C_olnfm{tr-1}")
                for c in range(NF):
                    ins = T.transpose(Ct["ps_tp"][1][:, 128 * c:128 * (c + 1)],
                                      Ct["oln"][tr % 2][:, 128 * c:128 * (c + 1)],
                                      idb_s[:])
                E.inc(ins, "CT", 1, f"C_tpln{tr}")
            if t < NT:
                E.wait(f"C_qfm{t}", "state_ready",
                       None if t < 1 else f"C_ot{t-1}")
                for p in range(8):
                    T.ldweights(Ct["qfm"][:, 128 * p:128 * (p + 1)])
                    ins = T.matmul(Ct["ps_out"][:, 128 * p:128 * (p + 1)],
                                   Ct["qfm"][:, 128 * p:128 * (p + 1)],
                                   Ct["state"][:, 128 * p:128 * (p + 1)],
                                   start=True, stop=True)
                E.inc(ins, "CT", 1, f"C_out{t}")
            if t >= 3:
                tr = t - 3
                E.wait(f"C_olnfm{tr}", None if tr < 1 else f"C_y{tr-1}")
                for c in range(NF):
                    T.ldweights(Ct["olnfm"][:, 128 * c:128 * (c + 1)])
                    for h in range(2):
                        ins = T.matmul(
                            Ct["ps_yw"][:, 512 * h:512 * (h + 1)],
                            Ct["olnfm"][:, 128 * c:128 * (c + 1)],
                            Ct["Wo"][:, C * c + 512 * h:C * c + 512 * (h + 1)],
                            start=(c == 0), stop=(c == NF - 1))
                E.inc(ins, "CT", 1, f"C_wo{tr}")

    body3 = [("sync", body3_sync), ("gpsimd", body3_gpsimd),
             ("vector", body3_vector), ("scalar", body3_scalar),
             ("tensor", body3_tensor)]

    class _Slc:
        def __getitem__(self, k):
            return self

    for key in ("w1q", "w2q", "n1w", "n2w", "hact", "u", "xh", "xq",
                "xqfm", "xq2tm", "xq2fm", "qs1", "beta", "alpha",
                "qs2", "sqacc", "geB", "sgeB", "tB", "ps_h", "ps_tpB",
                "ps_y2"):
        Bt[key] = _Slc()
    for key in ("Wo", "lng", "lnb", "bo", "qb", "ypb", "qfm", "ot", "osc",
                "oln", "olnfm", "yb", "so", "so2", "ve", "sve", "tC", "m2",
                "sres", "state", "ps_tp", "ps_out", "ps_yw"):
        Ct[key] = _Slc()
    gseq_plan = {}
    for nm, fn in body1 + body2 + body3:
        fn(_EB(sched, sems, counts, None, gseq_plan,
               serial=nm in ("vector", "scalar")))
    Bt.clear()
    Ct.clear()

    gseq_emit = {}
    with nc.Block("blk1") as blk:
        for nm, fn in body1:
            getattr(blk, nm)(lambda eng, fn=fn, nm=nm: fn(
                _EB(sched, sems, None, eng, gseq_emit,
                    serial=nm in ("vector", "scalar"))))
    es_a.close()
    alloc_b()
    with nc.Block("blk2") as blk:
        for nm, fn in body2:
            getattr(blk, nm)(lambda eng, fn=fn, nm=nm: fn(
                _EB(sched, sems, None, eng, gseq_emit,
                    serial=nm in ("vector", "scalar"))))
    es_b.close()
    alloc_c()
    with nc.Block("blk3") as blk:
        for nm, fn in body3:
            getattr(blk, nm)(lambda eng, fn=fn, nm=nm: fn(
                _EB(sched, sems, None, eng, gseq_emit,
                    serial=nm in ("vector", "scalar"))))
    es_c.close()
    es.close()
    return nc


def kernel(L0_attn_x, L0_attn_Wq, L0_attn_bq, L0_attn_Wk, L0_attn_bk,
           L0_attn_Wv, L0_attn_bv, L0_attn_Wo, L0_attn_bo, L0_attn_ln_g,
           L0_attn_ln_b, L0_ffn_norm1_w, L0_ffn_w1, L0_ffn_act_scale,
           L0_ffn_act_bias, L0_ffn_norm2_w, L0_ffn_w2):
    NT = 32
    TPC = NT * 128
    x = np.ascontiguousarray(np.asarray(L0_attn_x, dtype=np.float32)
                             ).reshape(8 * TPC, C)
    f32 = lambda a: np.ascontiguousarray(np.asarray(a, dtype=np.float32))
    row = lambda a: f32(a).reshape(1, -1)
    common = {
        "Wq": f32(L0_attn_Wq), "Wo": f32(L0_attn_Wo),
        "Wk0": f32(np.asarray(L0_attn_Wk)[0]),
        "Wk1": f32(np.asarray(L0_attn_Wk)[1]),
        "Wv0": f32(np.asarray(L0_attn_Wv)[0]),
        "Wv1": f32(np.asarray(L0_attn_Wv)[1]),
        "w1t": f32(np.asarray(L0_ffn_w1).T),
        "w2t": f32(np.asarray(L0_ffn_w2).T),
        "bq": row(L0_attn_bq), "bk0": row(np.asarray(L0_attn_bk)[0]),
        "bk1": row(np.asarray(L0_attn_bk)[1]),
        "bv0": row(np.asarray(L0_attn_bv)[0]),
        "bv1": row(np.asarray(L0_attn_bv)[1]), "bo": row(L0_attn_bo),
        "lng": row(L0_attn_ln_g), "lnb": row(L0_attn_ln_b),
        "n1w": row(L0_ffn_norm1_w), "n2w": row(L0_ffn_norm2_w),
        "acts": f32(L0_ffn_act_scale).reshape(1, 1),
        "actb": f32(L0_ffn_act_bias).reshape(1, 1),
    }
    in_maps = [dict(common, x=np.ascontiguousarray(x[c * TPC:(c + 1) * TPC]))
               for c in range(8)]
    nc = build_nc(NT)
    res = run_bass_kernel_spmd(nc, in_maps, core_ids=list(range(8)))
    y = np.concatenate([np.asarray(res.results[c]["y"]) for c in range(8)],
                       axis=0)
    return y.reshape(4, 8192, C)
